# version 12
# baseline (speedup 1.0000x reference)
import numpy as np
import ml_dtypes
from contextlib import ExitStack
import concourse.bass as bass
import concourse.mybir as mybir
from concourse.bass_utils import run_bass_kernel_spmd

F32 = mybir.dt.float32
BF16 = mybir.dt.bfloat16
AF = mybir.ActivationFunctionType
ALU = mybir.AluOpType
AXX = mybir.AxisListType.X
BFNP = ml_dtypes.bfloat16

D_MODEL = 1024
D_INNER = 2048
B = 4
S = 4096
T = 2048
MAGIC = 12582912.0
LN_EPS = 1e-5




def _ap_names(args, kwargs):
    names = []
    for a in list(args) + list(kwargs.values()):
        t = getattr(a, "tensor", None)
        if t is not None:
            names.append(t.name)
    return names


class EP:

    _OPS = ("tensor_copy", "tensor_tensor", "tensor_add", "tensor_sub",
            "tensor_mul", "tensor_max", "tensor_scalar", "tensor_single_scalar",
            "tensor_scalar_add", "tensor_scalar_sub", "tensor_scalar_mul",
            "tensor_scalar_max", "tensor_scalar_min", "tensor_relu",
            "tensor_reduce", "reduce_max", "reduce_sum", "scalar_tensor_tensor",
            "tensor_tensor_scan", "reciprocal", "select",
            "activation", "memset", "copy", "mul", "add", "iota")

    def __init__(self):
        self._e = None
        self._written = set()
        self._read = set()

    def bind(self, eng):
        self._e = eng
        return self

    def __getattr__(self, name):
        if name.startswith("_") or name in ("bind", "barrier"):
            raise AttributeError(name)
        f = getattr(self._e, name)
        if name not in EP._OPS:
            return f

        def wrap(*args, **kwargs):
            names = _ap_names(args, kwargs)
            out = names[0] if names else None
            ins = names[1:]
            if (out in self._written or out in self._read
                    or any(n in self._written for n in ins)):
                self._e.drain()
                self._written.clear()
                self._read.clear()
            r = f(*args, **kwargs)
            if out is not None:
                self._written.add(out)
            self._read.update(ins)
            return r
        return wrap

    def barrier(self):
        self._e.drain()
        self._written.clear()
        self._read.clear()


class Ctr:

    def __init__(self, sem, step=1):
        self.sem = sem
        self.n = 0
        self.step = step

    def inc(self, inst):
        self.n += self.step
        return inst.then_inc(self.sem, self.step)

    def ginc(self, eng, inst, depth=6):
        if self.n >= depth * self.step:
            eng.wait_ge(self.sem, self.n - (depth - 1) * self.step)
        return self.inc(inst)

    def mark(self, eng):
        ep = eng if not isinstance(eng, EP) else eng._e
        self.n += self.step
        return ep.drain().then_inc(self.sem, self.step)


def build_stage1():
    nc = bass.Bass("TRN2", target_bir_lowering=False)
    KC, OC, NT, TS = 8, 16, 4, 512
    xh = nc.dram_tensor("xh", [T + 2, D_MODEL], F32, kind="ExternalInput")
    wconv = nc.dram_tensor("wconv", [3 * D_MODEL, D_INNER], F32, kind="ExternalInput")
    wz = nc.dram_tensor("wz", [D_MODEL, D_INNER], F32, kind="ExternalInput")
    wg = nc.dram_tensor("wg", [D_INNER, D_INNER], F32, kind="ExternalInput")
    wxp = nc.dram_tensor("wxp", [D_INNER, 32], F32, kind="ExternalInput")
    biases = nc.dram_tensor("biases", [128, 49], F32, kind="ExternalInput")
    identw = nc.dram_tensor("identw", [128, 128], F32, kind="ExternalInput")

    o_u = nc.dram_tensor("o_u", [D_INNER, T], BF16, kind="ExternalOutput")
    o_sigz = nc.dram_tensor("o_sigz", [D_INNER, T], BF16, kind="ExternalOutput")
    o_bcdt = nc.dram_tensor("o_bcdt", [33, T], F32, kind="ExternalOutput")

    with ExitStack() as ctx:
        e = ctx.enter_context
        xnf = e(nc.sbuf_tensor("xnf", [128, KC * (T + 2)], BF16))
        ident = e(nc.sbuf_tensor("ident", [128, 128], BF16))
        onesb = e(nc.sbuf_tensor("onesb", [128, 1], BF16))
        wxpb = e(nc.sbuf_tensor("wxpb", [128, 16 * 32], BF16))
        biasv = e(nc.sbuf_tensor("biasv", [128, 49], F32))
        bcv = e(nc.sbuf_tensor("bcv", [33, T], F32))
        scr = {k: e(nc.sbuf_tensor("scr_" + k, [128, 17], F32))
               for k in ("mean", "var", "t1", "t2", "t3")}
        scr["sq"] = e(nc.sbuf_tensor("scr_sq", [128, D_MODEL], F32))
        ps = [e(nc.psum_tensor(f"ps{i}", [128, TS], F32)) for i in range(4)]
        psz = [e(nc.psum_tensor(f"psz{i}", [128, TS], F32)) for i in range(2)]
        pst = [e(nc.psum_tensor(f"pst{i}", [128, 1024], BF16)) for i in range(2)]
        sem_ld = e(nc.semaphore("s_ld"))
        sem_w0 = e(nc.semaphore("s_w0"))
        sem_w1 = e(nc.semaphore("s_w1"))
        sem_misc = e(nc.semaphore("s_misc"))
        sem_v = e(nc.semaphore("s_v"))
        sem_s = e(nc.semaphore("s_s"))
        sem_t = e(nc.semaphore("s_t"))
        sem_st = e(nc.semaphore("s_st"))
        sem_so = e(nc.semaphore("s_so"))
        LD = Ctr(sem_ld, 16)
        WL = [Ctr(sem_w0, 16), Ctr(sem_w1, 16)]
        V = Ctr(sem_v)
        SS = Ctr(sem_s)
        TT = Ctr(sem_t)
        ST = Ctr(sem_st, 16)
        SO = Ctr(sem_so, 16)
        epv, eps, ept, epg = EP(), EP(), EP(), EP()

        with nc.sbuf_tensor("xt", [128, 17 * D_MODEL], F32) as xt, \
             nc.sbuf_tensor("xnt", [128, 17 * D_MODEL], BF16) as xnt:
            bstk = ExitStack()
            blk = bstk.enter_context(nc.Block())

            @blk.sync
            def _(sync):
                for i in range(16):
                    LD.inc(sync.dma_start(xt[:, i * D_MODEL:(i + 1) * D_MODEL],
                                          xh[2 + i * 128: 2 + (i + 1) * 128, :]))
                LD.inc(sync.dma_start(xt[0:2, 16 * D_MODEL:17 * D_MODEL],
                                      xh[0:2, :]))
                LD.inc(sync.dma_start(biasv[:], biases.ap()))
            ld_all = LD.n

            @blk.gpsimd
            def _(g):
                g.dma_start(
                    wxpb[:].rearrange("p (kc s) -> p kc s", kc=16),
                    wxp.ap().rearrange("(kc p) s -> p kc s", p=128)).then_inc(sem_misc, 16)
                g.wait_ge(sem_misc, 16)
                g.dma_start(ident[:], identw.ap()).then_inc(sem_misc, 16)

            @blk.vector
            def _(v):
                v = epv.bind(v)
                v.wait_ge(sem_ld, ld_all)
                v.memset(onesb[:], 1.0)
                for i in range(17):
                    P = 128 if i < 16 else 2
                    xti = xt[:, i * D_MODEL:(i + 1) * D_MODEL]
                    c = lambda nm: scr[nm][:P, i:i + 1]
                    v.tensor_reduce(c("mean"), xti[:P, :], AXX, ALU.add)
                    v.tensor_scalar(c("mean"), c("mean"), 1.0 / 1024.0, None, ALU.mult)
                    v.tensor_tensor(scr["sq"][:P, :], xti[:P, :], xti[:P, :], ALU.mult)
                    v.tensor_reduce(c("var"), scr["sq"][:P, :], AXX, ALU.add)
                    v.tensor_scalar(c("var"), c("var"), 1.0 / 1024.0, None, ALU.mult)
                    v.tensor_tensor(c("t1"), c("mean"), c("mean"), ALU.mult)
                    v.tensor_sub(c("var"), c("var"), c("t1"))
                    v.tensor_scalar(c("var"), c("var"), LN_EPS, None, ALU.add)
                    v.reciprocal(c("t1"), c("var"))
                V.mark(v)
                v.wait_ge(sem_s, 1)
                for i in range(17):
                    P = 128 if i < 16 else 2
                    xti = xt[:, i * D_MODEL:(i + 1) * D_MODEL]
                    c = lambda nm: scr[nm][:P, i:i + 1]
                    v.tensor_tensor(c("t3"), c("t2"), c("t2"), ALU.mult)
                    v.tensor_tensor(c("t3"), c("t3"), c("var"), ALU.mult)
                    v.tensor_scalar(c("t3"), c("t3"), -0.5, 1.5, ALU.mult, ALU.add)
                    v.tensor_tensor(c("t2"), c("t2"), c("t3"), ALU.mult)
                    v.tensor_scalar(xnt[:P, i * D_MODEL:(i + 1) * D_MODEL],
                                    xti[:P, :], c("mean"), c("t2"),
                                    ALU.subtract, ALU.mult)
                V.mark(v)
            v_ln = V.n

            @blk.scalar
            def _(s):
                s = eps.bind(s)
                s.wait_ge(sem_v, 1)
                for i in range(17):
                    P = 128 if i < 16 else 2
                    s.activation(scr["t2"][:P, i:i + 1], scr["t1"][:P, i:i + 1],
                                 AF.Sqrt)
                SS.mark(s)

            @blk.tensor
            def _(t):
                t.wait_ge(sem_v, v_ln)
                t.wait_ge(sem_misc, 32)
                n = 0
                for i in range(17):
                    P = 128 if i < 16 else 2
                    for kc in range(KC):
                        if n >= 2:
                            t.wait_ge(sem_s, 1 + (n - 1))
                        t.transpose(pst[n % 2][:, 0:P],
                                    xnt[:P, i * D_MODEL + kc * 128:
                                        i * D_MODEL + (kc + 1) * 128],
                                    ident[:P, :P])
                        TT.mark(t)
                        n += 1

            @blk.scalar
            def _(s):
                s = eps.bind(s)
                n = 0
                for i in range(17):
                    P = 128 if i < 16 else 2
                    for kc in range(KC):
                        s.wait_ge(sem_t, n + 1)
                        dst = (2 + i * 128) if i < 16 else 0
                        SS.inc(s.activation(
                            xnf[:, kc * (T + 2) + dst:kc * (T + 2) + dst + P],
                            pst[n % 2][:, 0:P], AF.Copy))
                        n += 1
            s_tr = SS.n
            t_tr = TT.n
            bstk.close()

        wb = e(nc.sbuf_tensor("wb", [128, 2 * 24 * 128], BF16))
        wgb = e(nc.sbuf_tensor("wgb", [128, 2 * 16 * 128], BF16))
        xcv = e(nc.sbuf_tensor("xcv", [128, OC * T], BF16))
        uf = e(nc.sbuf_tensor("uf", [128, OC * T], BF16))

        bstk2 = ExitStack()
        blk2 = bstk2.enter_context(nc.Block())

        def wcol(buf, slot):
            return wb[:, buf * 3072 + slot * 128: buf * 3072 + (slot + 1) * 128]

        @blk2.gpsimd
        def _(g):
            for oc in range(OC):
                buf = oc % 2
                if oc >= 2:
                    g.wait_ge(sem_t, t_tr + (oc - 1) * NT)
                WL[buf].inc(g.dma_start(
                    wb[:, buf * 3072:buf * 3072 + 3072]
                    .rearrange("p (a o) -> p a o", a=24),
                    wconv.ap().rearrange("(a p) o -> p a o", p=128)
                    [:, :, oc * 128:(oc + 1) * 128]))
        wl_conv = [WL[0].n, WL[1].n]

        @blk2.tensor
        def _(t):
            t.wait_ge(sem_s, s_tr)
            for oc in range(OC):
                buf = oc % 2
                t.wait_ge([sem_w0, sem_w1][buf], 16 * (oc // 2 + 1))
                for ts in range(NT):
                    g_i = oc * NT + ts
                    if g_i >= 4:
                        t.wait_ge(sem_s, s_tr + (g_i - 4) + 1)
                    first = True
                    for tap in range(3):
                        for kc in range(KC):
                            rhs = xnf[:, kc * (T + 2) + tap + ts * TS:
                                      kc * (T + 2) + tap + ts * TS + TS]
                            t.matmul(ps[g_i % 4][:], wcol(buf, tap * KC + kc),
                                     rhs, start=first,
                                     stop=(tap == 2 and kc == KC - 1))
                            first = False
                    TT.mark(t)
        t_conv = TT.n

        @blk2.scalar
        def _(s):
            s = eps.bind(s)
            for g_i in range(OC * NT):
                oc, ts = divmod(g_i, NT)
                s.wait_ge(sem_t, t_tr + g_i + 1)
                SS.inc(s.activation(
                    xcv[:, oc * T + ts * TS: oc * T + ts * TS + TS],
                    ps[g_i % 4][:], AF.Identity, bias=biasv[:, oc:oc + 1]))
        s_conv = SS.n

        bstk2.close()
        bstk3 = ExitStack()
        blk3 = bstk3.enter_context(nc.Block())

        @blk3.gpsimd
        def _(g):
            for oc in range(OC):
                buf = oc % 2
                if oc >= 2:
                    g.wait_ge(sem_t, t_conv + (oc - 1) * NT)
                WL[buf].inc(g.dma_start(
                    wb[:, buf * 3072:buf * 3072 + 1024]
                    .rearrange("p (a o) -> p a o", a=8),
                    wz.ap().rearrange("(a p) o -> p a o", p=128)
                    [:, :, oc * 128:(oc + 1) * 128]))
        wl_z = [WL[0].n, WL[1].n]

        @blk3.tensor
        def _(t):
            for oc in range(OC):
                buf = oc % 2
                t.wait_ge([sem_w0, sem_w1][buf],
                          wl_conv[buf] + 16 * (oc // 2 + 1))
                for ts in range(NT):
                    g_i = oc * NT + ts
                    if g_i >= 2:
                        t.wait_ge(sem_s, s_conv + (g_i - 2) + 1)
                    for kc in range(KC):
                        rhs = xnf[:, kc * (T + 2) + 2 + ts * TS:
                                  kc * (T + 2) + 2 + ts * TS + TS]
                        t.matmul(psz[g_i % 2][:], wcol(buf, kc), rhs,
                                 start=(kc == 0), stop=(kc == KC - 1))
                    TT.mark(t)
        t_z = TT.n

        @blk3.scalar
        def _(s):
            s = eps.bind(s)
            for g_i in range(OC * NT):
                oc, ts = divmod(g_i, NT)
                s.wait_ge(sem_t, t_conv + g_i + 1)
                SS.inc(s.activation(
                    uf[:, oc * T + ts * TS: oc * T + ts * TS + TS],
                    psz[g_i % 2][:], AF.Sigmoid,
                    bias=biasv[:, 32 + oc:32 + oc + 1]))
        s_z = SS.n

        @blk3.gpsimd
        def _(g):
            for oc in range(OC):
                g.wait_ge(sem_s, s_conv + (oc + 1) * NT)
                ST.inc(g.dma_start(o_sigz[oc * 128:(oc + 1) * 128, :],
                                   uf[:, oc * T:(oc + 1) * T]))
        st_sigz = ST.n

        bstk3.close()
        bstk4 = ExitStack()
        blk4 = bstk4.enter_context(nc.Block())

        def wgcol(buf, kc):
            return wgb[:, buf * 2048 + kc * 128: buf * 2048 + (kc + 1) * 128]

        @blk4.gpsimd
        def _(g):
            for oc in range(OC):
                buf = oc % 2
                if oc >= 2:
                    g.wait_ge(sem_t, t_z + (oc - 1) * NT)
                WL[buf].inc(g.dma_start(
                    wgb[:, buf * 2048:buf * 2048 + 2048]
                    .rearrange("p (a o) -> p a o", a=16),
                    wg.ap().rearrange("(a p) o -> p a o", p=128)
                    [:, :, oc * 128:(oc + 1) * 128]))
        wl_g = [WL[0].n, WL[1].n]

        @blk4.tensor
        def _(t):
            t.wait_ge(sem_st, st_sigz)
            for oc in range(OC):
                buf = oc % 2
                t.wait_ge([sem_w0, sem_w1][buf],
                          wl_z[buf] + 16 * (oc // 2 + 1))
                for ts in range(NT):
                    g_i = oc * NT + ts
                    if g_i >= 2:
                        t.wait_ge(sem_s, s_z + (g_i - 2) + 1)
                    for kc in range(16):
                        t.matmul(psz[g_i % 2][:], wgcol(buf, kc),
                                 xcv[:, kc * T + ts * TS: kc * T + ts * TS + TS],
                                 start=(kc == 0), stop=(kc == 15))
                    TT.mark(t)
            for ts in range(NT):
                if ts >= 1:
                    t.wait_ge(sem_s, s_z + OC * NT + 2 * ts)
                for kc in range(16):
                    t.matmul(ps[0][0:32, :], wxpb[:, kc * 32:(kc + 1) * 32],
                             xcv[:, kc * T + ts * TS: kc * T + ts * TS + TS],
                             start=(kc == 0), stop=(kc == 15))
                TT.mark(t)
                for kc in range(16):
                    t.matmul(ps[1][0:1, :], onesb[:],
                             xcv[:, kc * T + ts * TS: kc * T + ts * TS + TS],
                             start=(kc == 0), stop=(kc == 15))
                TT.mark(t)
        t_gate = t_z + OC * NT
        t_xp = TT.n

        @blk4.scalar
        def _(s):
            s = eps.bind(s)
            for g_i in range(OC * NT):
                oc, ts = divmod(g_i, NT)
                s.wait_ge(sem_t, t_z + g_i + 1)
                SS.inc(s.activation(
                    uf[:, oc * T + ts * TS: oc * T + ts * TS + TS],
                    psz[g_i % 2][:], AF.Sigmoid,
                    bias=biasv[:, 16 + oc:16 + oc + 1]))
            for ts in range(NT):
                s.wait_ge(sem_t, t_gate + 2 * ts + 1)
                SS.inc(s.activation(bcv[0:32, ts * TS:(ts + 1) * TS],
                                    ps[0][0:32, :], AF.Identity,
                                    bias=biasv[0:32, 48:49]))
                s.wait_ge(sem_t, t_gate + 2 * ts + 2)
                SS.inc(s.activation(bcv[32:33, ts * TS:(ts + 1) * TS],
                                    ps[1][0:1, :], AF.Sigmoid,
                                    scale=1.0 / 2048.0))
        s_gate = s_z + OC * NT
        s_all = SS.n

        @blk4.vector
        def _(v):
            v = epv.bind(v)
            for oc in range(OC):
                v.wait_ge(sem_s, s_z + (oc + 1) * NT)
                v.tensor_tensor(uf[:, oc * T:(oc + 1) * T],
                                xcv[:, oc * T:(oc + 1) * T],
                                uf[:, oc * T:(oc + 1) * T], ALU.mult)
                V.mark(v)
            v.wait_ge(sem_s, s_all)
            v.tensor_scalar(bcv[32:33, :], bcv[32:33, :], 0.001, None, ALU.add)
            V.mark(v)
        v_all = V.n

        bstk4.close()
        bstk5 = ExitStack()
        blk5 = bstk5.enter_context(nc.Block())

        @blk5.sync
        def _(sync):
            for oc in range(OC):
                sync.wait_ge(sem_v, v_ln + oc + 1)
                SO.inc(sync.dma_start(o_u[oc * 128:(oc + 1) * 128, :],
                                      uf[:, oc * T:(oc + 1) * T]))
            sync.wait_ge(sem_v, v_all)
            SO.inc(sync.dma_start(o_bcdt.ap(), bcv[:]))

        @blk5.gpsimd
        def _(g):
            g.wait_ge(sem_st, ST.n)
            g.wait_ge(sem_so, SO.n)
        bstk5.close()
    return nc




def build_stage2():
    nc = bass.Bass("TRN2", target_bir_lowering=False)
    L = 127
    NCH = (S + L - 1) // L
    NF = 16 * 1024
    NSS = 16
    NEV = 5

    SPAD = S + 128
    u_d = nc.dram_tensor("u_d", [1024, SPAD], BF16, kind="ExternalInput")
    bcdt = nc.dram_tensor("bcdt", [33, S], F32, kind="ExternalInput")
    a_half = nc.dram_tensor("a_half", [1024, 16], F32, kind="ExternalInput")
    d_half = nc.dram_tensor("d_half", [1, 1024], F32, kind="ExternalInput")
    ltri_c = nc.dram_tensor("ltri_c", [128, 128], BF16, kind="ExternalInput")
    identw = nc.dram_tensor("identw", [128, 128], F32, kind="ExternalInput")
    zrowd = nc.dram_tensor("zrowd", [1, NF], BF16, kind="ExternalInput")
    onesd = nc.dram_tensor("onesd", [1, 2048], BF16, kind="ExternalInput")
    o_y = nc.dram_tensor("o_y", [S, 1024], BF16, kind="ExternalOutput")
    asp_d = nc.dram_tensor("asp_d", [16, 1024], F32, kind="Internal")

    def tok(k):
        t0 = k * L
        return t0, min(L, S - t0)

    with ExitStack() as ctx:
        e = ctx.enter_context
        arep = e(nc.sbuf_tensor("arep", [128, NF], BF16))
        utok = e(nc.sbuf_tensor("utok", [128, 2 * 1024], BF16))
        ycb = e(nc.sbuf_tensor("ycb", [128, 2 * 1024], BF16))
        durep = e(nc.sbuf_tensor("durep", [128, 1024], BF16))
        ltri = e(nc.sbuf_tensor("ltri", [128, 128], BF16))
        ltris = e(nc.sbuf_tensor("ltris", [128, 2 * 16 * 128], BF16))
        identf = e(nc.sbuf_tensor("identf", [128, 128], F32))
        onesr = e(nc.sbuf_tensor("onesr", [1, 128], F32))
        rows36 = e(nc.sbuf_tensor("rows36", [36, S], F32))
        cols = e(nc.sbuf_tensor("cols", [128, 128], F32))
        bmdt = e(nc.sbuf_tensor("bmdt", [128, 2 * 16], F32))
        pst = e(nc.psum_tensor("pst", [128, 512], F32))
        sem_ld = e(nc.semaphore("s_ld"))
        sem_ld2 = e(nc.semaphore("s_ld2"))
        sem_ut = [e(nc.semaphore(f"s_ut{i}")) for i in range(3)]
        sem_sl = e(nc.semaphore("s_sl"))
        sem_g = e(nc.semaphore("s_g"))
        sem_v = e(nc.semaphore("s_v"))
        sem_s = e(nc.semaphore("s_s"))
        sem_t = e(nc.semaphore("s_t"))
        sem_so = [e(nc.semaphore(f"s_so{i}")) for i in range(2)]
        LD = Ctr(sem_ld, 16)
        LD2 = Ctr(sem_ld2, 16)
        GD = Ctr(sem_g, 16)
        V = Ctr(sem_v)
        SS = Ctr(sem_s)
        TT = Ctr(sem_t)
        epv, eps = EP(), EP()

        pstk = ExitStack()
        pe = pstk.enter_context
        asp = pe(nc.sbuf_tensor("asp", [128, 8 * 16], F32))
        dvrow = pe(nc.sbuf_tensor("dvrow", [1, 1024], F32))
        aspt = pe(nc.sbuf_tensor("aspt", [16, 1024], F32))
        dtr = pe(nc.sbuf_tensor("dtr", [1, S], F32))
        crow = pe(nc.sbuf_tensor("crow", [1, S], F32))
        ncp = pe(nc.sbuf_tensor("ncp", [1, S], F32))
        zrow = pe(nc.sbuf_tensor("zrow", [1, S], F32))
        psb = pe(nc.psum_tensor("psb", [128, 1024], F32))
        bstk = ExitStack()
        blk = bstk.enter_context(nc.Block())

        @blk.sync
        def _(sync):
            LD.inc(sync.dma_start(identf[:], identw.ap()))
            LD.inc(sync.dma_start(ltri[:], ltri_c.ap()))
            LD.inc(sync.dma_start(rows36[3:36, :], bcdt.ap()))
            LD.inc(sync.dma_start(dtr[:], bcdt[32:33, :]))
            LD.inc(sync.dma_start(dvrow[:], d_half.ap()))
            LD.inc(sync.dma_start(rows36[2:3, :], bcdt[32:33, :]))
            for i in range(8):
                LD.inc(sync.dma_start(asp[:, i * 16:(i + 1) * 16],
                                      a_half[i * 128:(i + 1) * 128, :]))
        ld_prep = LD.n

        @blk.vector
        def _(v):
            v = epv.bind(v)
            v.memset(onesr[:], 1.0)
            v.memset(zrow[:], 0.0)
            v.wait_ge(sem_s, 1)
            u_ = asp[:, 0:128]
            p_ = cols[:, 0:128]
            v.tensor_scalar(p_, u_, -1.0 / 6.0, 1.0 / 5.0, ALU.mult, ALU.add)
            for coef in (1.0 / 4.0, 1.0 / 3.0, 1.0 / 2.0, 1.0):
                v.tensor_tensor(p_, p_, u_, ALU.mult)
                v.tensor_scalar(p_, p_, -1.0, coef, ALU.mult, ALU.add)
            v.tensor_tensor(asp[:, 0:128], p_, u_, ALU.mult)
            V.mark(v)
            v.wait_ge(sem_ld, ld_prep)
            v.tensor_tensor_scan(crow[:], dtr[:], zrow[:], 0.0, ALU.add, ALU.add)
            v.memset(ncp[0:1, 0:1], 0.0)
            v.tensor_scalar(ncp[0:1, 1:S], crow[0:1, 0:S - 1], -1.0, None, ALU.mult)
            V.mark(v)

        @blk.scalar
        def _(s):
            s = eps.bind(s)
            s.wait_ge(sem_ld, ld_prep)
            for i in range(8):
                s.activation(asp[:, i * 16:(i + 1) * 16],
                             asp[:, i * 16:(i + 1) * 16], AF.Exp)
            SS.mark(s)

        @blk.tensor
        def _(t):
            t.wait_ge(sem_ld, ld_prep)
            t.wait_ge(sem_v, 1)
            for i in range(8):
                if i >= 1:
                    t.wait_ge(sem_s, 1 + i)
                t.transpose(pst[0:16, 0:128], asp[:, i * 16:(i + 1) * 16],
                            identf[:])
                t.drain().then_inc(sem_t, 1)
                TT.n += 1
            t.matmul(psb[:, 0:512], onesr[:], dvrow[0:1, 0:512],
                     start=True, stop=True)
            t.matmul(psb[:, 512:1024], onesr[:], dvrow[0:1, 512:1024],
                     start=True, stop=True)
            t.drain().then_inc(sem_t, 1)
            TT.n += 1
        tt_prep = TT.n

        @blk.scalar
        def _(s):
            s = eps.bind(s)
            for i in range(8):
                s.wait_ge(sem_t, i + 1)
                SS.inc(s.activation(aspt[:, i * 128:(i + 1) * 128],
                                    pst[0:16, 0:128], AF.Copy))
            s.wait_ge(sem_t, tt_prep)
            SS.inc(s.activation(durep[:], psb[:, 0:1024], AF.Copy))
        ss_prep = SS.n

        @blk.sync
        def _(sync):
            sync.wait_ge(sem_s, ss_prep)
            LD2.inc(sync.dma_start(asp_d.ap(), aspt[:]))
            sync.wait_ge(sem_v, 2)
            LD2.inc(sync.dma_start(rows36[0:1, :], ncp[:]))
            LD2.inc(sync.dma_start(rows36[1:2, :], crow[:]))
        ld_rows = LD2.n

        @blk.gpsimd
        def _(g):
            g.wait_ge(sem_ld2, ld_rows)
            g.dma_start(ltris[127:128, 0:2048], onesd.ap()).then_inc(sem_g, 16)
            g.dma_start(ltris[127:128, 2048:4096], onesd.ap()).then_inc(sem_g, 16)
            g.dma_start(cols[127:128, 19:35], onesd[0:1, 0:16]).then_inc(sem_g, 16)
            g.dma_start(cols[127:128, 59:75], onesd[0:1, 0:16]).then_inc(sem_g, 16)
            GD.n = 16 * 4
            for sidx in range(16):
                if sidx >= 1:
                    g.wait_ge(sem_g, GD.n)
                g.dma_start(arep[:, sidx * 1024:(sidx + 1) * 1024],
                            asp_d[sidx:sidx + 1, :].broadcast_to((128, 1024))
                            ).then_inc(sem_g, 16)
                GD.n += 16
        gd_arep = GD.n
        bstk.close()
        pstk.close()
        e0full = e(nc.sbuf_tensor("e0full", [128, 2 * NF], BF16))
        egb = e(nc.sbuf_tensor("egb", [128, NF], BF16))
        csb = e(nc.sbuf_tensor("csb", [128, NF], BF16))
        psw = [e(nc.psum_tensor(f"psw{i}", [128, 1024], F32)) for i in range(3)]

        rec_uses = []
        emitted = {}

        def use(key, val):
            rec_uses.append((key, val))
            return val

        def SBASE(k):
            return ss_prep + (0 if k == 0 else 2 + (k - 1) * 13)

        def s_einv_f(k):
            return SBASE(k) + (1 if k == 0 else 12)

        def s_e0_f(k):
            return SBASE(k) + (2 if k == 0 else 13)

        def s_ev_f(k, sx):
            assert sx >= NEV
            if k + 1 < NCH:
                return SBASE(k + 1) + (sx - NEV) + 1
            return ss_prep + 2 + (NCH - 1) * 13 + (sx - NEV) + 1

        def VBASE(k):
            return 2 + (0 if k == 0 else 34 + (k - 1) * 61)

        def v_cols_f(k):
            return VBASE(k) + 1

        def v_ltris_f(k, sx):
            return VBASE(k) + 2 + sx + 1

        def v_ev_f(k, sx):
            assert sx < NEV
            return VBASE(k + 1) + (18 if k + 1 < NCH else 0) + sx + 1

        def v_zdone_f(k):
            return VBASE(k + 1) + (39 if k + 1 < NCH else 21)

        def v_fin_f(k):
            return VBASE(k + 1) + (45 if k + 1 < NCH else 27)

        def v_g_f(k, sx):
            return VBASE(k) + (18 if k == 0 else 45) + sx + 1

        def TBASE(k):
            return tt_prep + 1 + k * 17

        def t_tr_f(k):
            return tt_prep + 1 if k == 0 else TBASE(k - 1) + 1

        def t_mm_f(k, sx):
            return TBASE(k) + (1 if k + 1 < NCH else 0) + sx + 1

        bstk1 = ExitStack()
        blk1 = bstk1.enter_context(nc.Block())

        @blk1.sync
        def _(sync):
            sync.dma_start(egb[127:128, :], zrowd.ap()).then_inc(sem_sl, 16)
            for k in range(NCH):
                t0, ln = tok(k)
                if k >= 2:
                    sync.wait_ge(sem_v, use(("vfin", k - 2), v_fin_f(k - 2)))
                sync.dma_start(
                    utok[:, (k % 2) * 1024:(k % 2) * 1024 + 1024],
                    u_d[:, t0:t0 + 128],
                    transpose=True).then_inc(sem_ut[k % 3], 16)
                if k >= 1:
                    sync.wait_ge(sem_s, use(("sev", k - 1, 15),
                                            s_ev_f(k - 1, 15)))
                    sync.wait_ge(sem_v, use(("vev", k - 1, NEV - 1),
                                            v_ev_f(k - 1, NEV - 1)))
                    sync.wait_ge(sem_t, use(("tmm", k - 1, 15),
                                            t_mm_f(k - 1, 15)))
                    sync.dma_start(egb[127:128, :],
                                   csb[127:128, :]).then_inc(sem_sl, 16)
                    pt0, pln = tok(k - 1)
                    sync.wait_ge(sem_v, use(("vfin", k - 1), v_fin_f(k - 1)))
                    sync.dma_start(
                        o_y[pt0:pt0 + pln, :],
                        ycb[0:pln, ((k - 1) % 2) * 1024:((k - 1) % 2) * 1024 + 1024]
                    ).then_inc(sem_so[(k - 1) % 2], 16)
            t0l, lnl = tok(NCH - 1)
            sync.wait_ge(sem_v, use(("vfin", NCH - 1), v_fin_f(NCH - 1)))
            sync.dma_start(
                o_y[t0l:t0l + lnl, :],
                ycb[0:lnl, ((NCH - 1) % 2) * 1024:((NCH - 1) % 2) * 1024 + 1024]
            ).then_inc(sem_so[(NCH - 1) % 2], 16)

        @blk1.scalar
        def _(s):
            s = eps.bind(s)
            s.wait_ge(sem_g, gd_arep)
            for k in range(NCH):
                t0, ln = tok(k)
                cb = (k % 2) * 40
                if k >= 1:
                    pt0, pln = tok(k - 1)
                    cbp = ((k - 1) % 2) * 40
                    for sx in range(NEV, NSS):
                        s.wait_ge(sem_t, use(("tmm", k - 1, sx),
                                             t_mm_f(k - 1, sx)))
                        SS.inc(s.activation(
                            csb[:, sx * 1024:(sx + 1) * 1024],
                            psw[sx % 3][:, :], AF.Identity,
                            scale=cols[:, cbp + 19 + sx:cbp + 20 + sx]))
                        emitted[("sev", k - 1, sx)] = SS.n
                s.wait_ge(sem_v, use(("vcols", k), v_cols_f(k)))
                SS.inc(s.activation(egb[0:ln, :], arep[0:ln, :], AF.Exp,
                                    scale=cols[0:ln, cb:cb + 1]))
                emitted[("seinv", k)] = SS.n
                if k >= 2:
                    s.wait_ge(sem_v, use(("vzd", k - 2), v_zdone_f(k - 2)))
                SS.inc(s.activation(e0full[0:ln, (k % 2) * NF:(k % 2) * NF + NF],
                                    arep[0:ln, :], AF.Exp,
                                    scale=cols[0:ln, cb + 1:cb + 2]))
                emitted[("se0", k)] = SS.n
            cbl = ((NCH - 1) % 2) * 40
            for sx in range(NEV, NSS):
                s.wait_ge(sem_t, use(("tmm", NCH - 1, sx),
                                     t_mm_f(NCH - 1, sx)))
                SS.inc(s.activation(csb[:, sx * 1024:(sx + 1) * 1024],
                                    psw[sx % 3][:, :], AF.Identity,
                                    scale=cols[:, cbl + 19 + sx:cbl + 20 + sx]))
                emitted[("sev", NCH - 1, sx)] = SS.n

        @blk1.vector
        def _(v):
            v = epv.bind(v)
            v.wait_ge(sem_g, gd_arep)
            for k in range(NCH + 1):
                if k < NCH:
                    t0, ln = tok(k)
                    cb = (k % 2) * 40
                    lb = (k % 2) * 2048
                    v.wait_ge(sem_t, use(("ttr", k), t_tr_f(k)))
                    V.inc(v.tensor_scalar(cols[0:ln, cb:cb + 36],
                                          pst[0:ln, 0:36], 1.0, None,
                                          ALU.mult))
                    emitted[("vcols", k)] = V.n
                    V.inc(v.tensor_scalar(bmdt[0:ln, (k % 2) * 16:(k % 2) * 16 + 16],
                                          cols[0:ln, cb + 3:cb + 19],
                                          cols[0:ln, cb + 35:cb + 36],
                                          None, ALU.mult))
                    for sx in range(16):
                        V.inc(v.tensor_scalar(
                            ltris[0:L, lb + sx * 128:lb + (sx + 1) * 128],
                            ltri[0:L, :],
                            bmdt[0:L, (k % 2) * 16 + sx:(k % 2) * 16 + sx + 1],
                            None, ALU.mult))
                        emitted[("vltris", k, sx)] = V.n
                if k >= 1:
                    kp = k - 1
                    pt0, pln = tok(kp)
                    parp = kp % 2
                    for sx in range(NEV):
                        v.wait_ge(sem_t, use(("tmm", kp, sx), t_mm_f(kp, sx)))
                        V.inc(v.tensor_scalar(
                            csb[:, sx * 1024:(sx + 1) * 1024],
                            psw[sx % 3][:, :],
                            cols[:, (parp * 40) + 19 + sx:(parp * 40) + 20 + sx],
                            None, ALU.mult))
                        emitted[("vev", kp, sx)] = V.n
                    v.wait_ge(sem_s, use(("se0", kp), s_e0_f(kp)))
                    for sx in range(16):
                        if sx >= NEV:
                            v.wait_ge(sem_s, use(("sev", kp, sx),
                                                 s_ev_f(kp, sx)))
                        V.inc(v.tensor_tensor(
                            csb[0:pln, sx * 1024:(sx + 1) * 1024],
                            csb[0:pln, sx * 1024:(sx + 1) * 1024],
                            e0full[0:pln, (kp % 2) * NF + sx * 1024:
                                   (kp % 2) * NF + (sx + 1) * 1024],
                            ALU.mult))
                    emitted[("vzd", kp)] = V.n
                    V.inc(v.tensor_tensor(csb[0:pln, 0:8192],
                                          csb[0:pln, 0:8192],
                                          csb[0:pln, 8192:16384], ALU.add))
                    V.inc(v.tensor_tensor(csb[0:pln, 0:4096],
                                          csb[0:pln, 0:4096],
                                          csb[0:pln, 4096:8192], ALU.add))
                    V.inc(v.tensor_tensor(csb[0:pln, 0:2048],
                                          csb[0:pln, 0:2048],
                                          csb[0:pln, 2048:4096], ALU.add))
                    V.inc(v.tensor_tensor(csb[0:pln, 0:1024],
                                          csb[0:pln, 0:1024],
                                          csb[0:pln, 1024:2048], ALU.add))
                    du = ycb[0:pln, parp * 1024:parp * 1024 + 1024]
                    if k >= 3:
                        v.wait_ge(sem_so[(k - 3) % 2], 16 * ((k - 3) // 2 + 1))
                    V.inc(v.tensor_tensor(
                        du, utok[0:pln, parp * 1024:parp * 1024 + 1024],
                        durep[0:pln, :], ALU.mult))
                    V.inc(v.tensor_tensor(du, du, csb[0:pln, 0:1024], ALU.add))
                    emitted[("vfin", kp)] = V.n
                if k < NCH:
                    par = k % 2
                    v.wait_ge(sem_ut[k % 3], 16 * (k // 3 + 1))
                    v.wait_ge(sem_s, use(("seinv", k), s_einv_f(k)))
                    for sx in range(16):
                        V.inc(v.tensor_tensor(
                            egb[0:ln, sx * 1024:(sx + 1) * 1024],
                            utok[0:ln, par * 1024:par * 1024 + 1024],
                            egb[0:ln, sx * 1024:(sx + 1) * 1024],
                            ALU.mult))
                        emitted[("vg", k, sx)] = V.n

        @blk1.tensor
        def _(t):
            t.wait_ge(sem_s, ss_prep)
            t.wait_ge(sem_ld2, ld_rows)
            t.wait_ge(sem_g, gd_arep)

            def tr(kk):
                t0k, lnk = tok(kk)
                if kk >= 1:
                    t.wait_ge(sem_v, use(("vcols", kk - 1),
                                         v_cols_f(kk - 1)))
                t.transpose(pst[0:lnk, 0:36], rows36[:, t0k:t0k + lnk],
                            identf[0:36, 0:36])
                t.drain().then_inc(sem_t, 1)
                TT.n += 1
                emitted[("ttr", kk)] = TT.n

            tr(0)
            for k in range(NCH):
                lb = (k % 2) * 2048
                if k + 1 < NCH:
                    tr(k + 1)
                for sx in range(NSS):
                    t.wait_ge(sem_v, use(("vltris", k, sx),
                                         v_ltris_f(k, sx)))
                    t.wait_ge(sem_v, use(("vg", k, sx), v_g_f(k, sx)))
                    if sx == 0:
                        t.wait_ge(sem_sl, 16 * (k + 1))
                    gidx = k * NSS + sx - 3
                    if gidx >= 0:
                        pk, ps = divmod(gidx, NSS)
                        if ps < NEV:
                            t.wait_ge(sem_v, use(("vev", pk, ps),
                                                 v_ev_f(pk, ps)))
                        else:
                            t.wait_ge(sem_s, use(("sev", pk, ps),
                                                 s_ev_f(pk, ps)))
                    t.matmul(psw[sx % 3][:, 0:512],
                             ltris[:, lb + sx * 128:lb + (sx + 1) * 128],
                             egb[:, sx * 1024:sx * 1024 + 512],
                             start=True, stop=True)
                    t.matmul(psw[sx % 3][:, 512:1024],
                             ltris[:, lb + sx * 128:lb + (sx + 1) * 128],
                             egb[:, sx * 1024 + 512:sx * 1024 + 1024],
                             start=True, stop=True)
                    t.drain().then_inc(sem_t, 1)
                    TT.n += 1
                    emitted[("tmm", k, sx)] = TT.n

        @blk1.gpsimd
        def _(g):
            g.wait_ge(sem_so[0], 16 * ((NCH + 1) // 2))
            g.wait_ge(sem_so[1], 16 * (NCH // 2))
        bstk1.close()
        for key, val in rec_uses:
            assert emitted[key] == val, (key, val, emitted[key])
    return nc


def build_stage3():
    nc = bass.Bass("TRN2", target_bir_lowering=False)
    KC2 = 16
    y_own = nc.dram_tensor("y_own", [T, D_INNER], BF16, kind="ExternalInput")
    sigz = nc.dram_tensor("sigz", [D_INNER, T], BF16, kind="ExternalInput")
    x_own = nc.dram_tensor("x_own", [T, D_MODEL], F32, kind="ExternalInput")
    wo_t = nc.dram_tensor("wo_t", [128, KC2 * D_MODEL], F32,
                          kind="ExternalInput")
    bo_r = nc.dram_tensor("bo_r", [1, D_MODEL], F32, kind="ExternalInput")
    o_x2 = nc.dram_tensor("o_x2", [T, D_MODEL], F32, kind="ExternalOutput")
    o_xn2 = nc.dram_tensor("o_xn2", [T, D_MODEL], F32, kind="ExternalOutput")
    o_amax = nc.dram_tensor("o_amax", [1, 1], F32, kind="ExternalOutput")

    with ExitStack() as ctx:
        e = ctx.enter_context
        sgz = e(nc.sbuf_tensor("sgz", [128, 3 * T], BF16))
        ofe = e(nc.sbuf_tensor("ofe", [128, KC2 * T], BF16))
        wob = e(nc.sbuf_tensor("wob", [128, KC2 * D_MODEL], BF16))
        bor = e(nc.sbuf_tensor("bor", [1, D_MODEL], F32))
        onesr = e(nc.sbuf_tensor("onesr", [1, 128], F32))
        xt = e(nc.sbuf_tensor("xt", [128, 3 * D_MODEL], F32))
        x2t = e(nc.sbuf_tensor("x2t", [128, 2 * D_MODEL], F32))
        xn2t = e(nc.sbuf_tensor("xn2t", [128, 2 * D_MODEL], F32))
        amx = e(nc.sbuf_tensor("amx", [128, 17], F32))
        amr = e(nc.sbuf_tensor("amr", [1, 128], F32))
        amo = e(nc.sbuf_tensor("amo", [1, 1], F32))
        identf = e(nc.sbuf_tensor("identf", [128, 128], F32))
        scr = {k: e(nc.sbuf_tensor("scr_" + k, [128, 16], F32))
               for k in ("mean", "var", "t1", "t2", "t3")}
        scr["sq"] = e(nc.sbuf_tensor("scr_sq", [128, D_MODEL], F32))
        psw = [e(nc.psum_tensor(f"psw{i}", [128, D_MODEL], F32))
               for i in range(3)]
        pst = e(nc.psum_tensor("pst", [128, 512], F32))
        sem_ld = e(nc.semaphore("s_ld"))
        sem_sg = [e(nc.semaphore(f"s_sg{i}")) for i in range(3)]
        sem_y = [e(nc.semaphore(f"s_y{i}")) for i in range(4)]
        sem_wo = [e(nc.semaphore(f"s_wo{i}")) for i in range(8)]
        sem_v = e(nc.semaphore("s_v"))
        sem_s = e(nc.semaphore("s_s"))
        sem_t = e(nc.semaphore("s_t"))
        sem_md = e(nc.semaphore("s_md"))
        sem_x = [e(nc.semaphore(f"s_x{i}")) for i in range(3)]
        sem_sox = [e(nc.semaphore(f"s_sox{i}")) for i in range(2)]
        sem_son = [e(nc.semaphore(f"s_son{i}")) for i in range(2)]
        identw = nc.dram_tensor("identw", [128, 128], F32, kind="ExternalInput")
        LD = Ctr(sem_ld, 16)
        V = Ctr(sem_v)
        SS = Ctr(sem_s)
        TT = Ctr(sem_t)
        epv, eps = EP(), EP()
        vA, vB, vAdd = {}, {}, {}
        tTile = {i: i + 1 for i in range(16)}

        bstk = ExitStack()
        blk = bstk.enter_context(nc.Block())

        def _sync_body(sync):
            LD.inc(sync.dma_start(bor[:], bo_r.ap()))
            LD.inc(sync.dma_start(identf[:], identw.ap()))
            for kc in range(KC2):
                if kc >= 4:
                    sync.wait_ge(sem_v, kc - 2)
                sync.dma_start(
                    ofe[:, kc * T:(kc + 1) * T],
                    y_own[:, kc * 128:(kc + 1) * 128],
                    transpose=True).then_inc(sem_y[kc % 4], 16)
            for i in range(16):
                if i >= 3:
                    sync.wait_ge(sem_v, vA[i - 3])
                sync.dma_start(
                    xt[:, (i % 3) * D_MODEL:(i % 3) * D_MODEL + D_MODEL],
                    x_own[i * 128:(i + 1) * 128, :]).then_inc(sem_x[i % 3], 16)
                if i >= 1:
                    j = i - 1
                    sync.wait_ge(sem_v, vA[j])
                    sync.dma_start(
                        o_x2[j * 128:(j + 1) * 128, :],
                        x2t[:, (j % 2) * D_MODEL:(j % 2 + 1) * D_MODEL]
                    ).then_inc(sem_sox[j % 2], 16)
                    sync.wait_ge(sem_v, vB[j])
                    sync.dma_start(
                        o_xn2[j * 128:(j + 1) * 128, :],
                        xn2t[:, (j % 2) * D_MODEL:(j % 2 + 1) * D_MODEL]
                    ).then_inc(sem_son[j % 2], 16)
            sync.wait_ge(sem_v, vA[15])
            sync.dma_start(o_x2[15 * 128:T, :],
                           x2t[:, (15 % 2) * D_MODEL:(15 % 2 + 1) * D_MODEL]
                           ).then_inc(sem_sox[15 % 2], 16)
            sync.wait_ge(sem_v, vB[15])
            sync.dma_start(o_xn2[15 * 128:T, :],
                           xn2t[:, (15 % 2) * D_MODEL:(15 % 2 + 1) * D_MODEL]
                           ).then_inc(sem_son[15 % 2], 16)

        @blk.gpsimd
        def _(g):
            for kc in range(KC2):
                if kc >= 3:
                    g.wait_ge(sem_v, kc - 1)
                g.dma_start(sgz[:, (kc % 3) * T:(kc % 3 + 1) * T],
                            sigz[kc * 128:(kc + 1) * 128, :]
                            ).then_inc(sem_sg[kc % 3], 16)
                if 2 <= kc < 10:
                    j = kc - 2
                    g.dma_start(wob[:, j * 2048:(j + 1) * 2048],
                                wo_t[:, j * 2048:(j + 1) * 2048]
                                ).then_inc(sem_wo[j], 16)

        @blk.vector
        def _(v):
            v = epv.bind(v)
            v.memset(onesr[:], 1.0)
            V.mark(v)
            v.wait_ge(sem_ld, 2 * 16)
            for kc in range(KC2):
                v.wait_ge(sem_y[kc % 4], 16 * (kc // 4 + 1))
                v.wait_ge(sem_sg[kc % 3], 16 * (kc // 3 + 1))
                V.inc(v.tensor_tensor(ofe[:, kc * T:(kc + 1) * T],
                                      ofe[:, kc * T:(kc + 1) * T],
                                      sgz[:, (kc % 3) * T:(kc % 3 + 1) * T],
                                      ALU.mult))
            v_o = V.n
            for i in range(17):
                if i < 16:
                    par = (i % 2) * D_MODEL
                    pw = psw[i % 3]
                    v.wait_ge(sem_t, tTile[i])
                    v.wait_ge(sem_x[i % 3], 16 * (i // 3 + 1))
                    if i >= 2:
                        v.wait_ge(sem_sox[(i - 2) % 2], 16 * ((i - 2) // 2 + 1))
                    xti = x2t[:, par:par + D_MODEL]
                    par3 = (i % 3) * D_MODEL
                    v.tensor_tensor(x2t[:, par:par + 512],
                                    xt[:, par3:par3 + 512],
                                    pw[:, 0:512], ALU.add)
                    v.tensor_tensor(x2t[:, par + 512:par + D_MODEL],
                                    xt[:, par3 + 512:par3 + D_MODEL],
                                    pw[:, 512:1024], ALU.add)
                    V.mark(v)
                    vAdd[i] = V.n
                    c = lambda nm: scr[nm][:, i:i + 1]
                    v.tensor_reduce(c("mean"), xti, AXX, ALU.add)
                    v.tensor_scalar(c("mean"), c("mean"), 1.0 / 1024.0, None,
                                    ALU.mult)
                    v.tensor_tensor(scr["sq"][:, :], xti, xti, ALU.mult)
                    v.tensor_reduce(c("var"), scr["sq"][:, :], AXX, ALU.add)
                    v.tensor_scalar(c("var"), c("var"), 1.0 / 1024.0, None,
                                    ALU.mult)
                    v.tensor_tensor(c("t1"), c("mean"), c("mean"), ALU.mult)
                    v.tensor_sub(c("var"), c("var"), c("t1"))
                    v.tensor_scalar(c("var"), c("var"), LN_EPS, None, ALU.add)
                    v.reciprocal(c("t1"), c("var"))
                    V.mark(v)
                    vA[i] = V.n
                if i >= 1:
                    j = i - 1
                    parj = (j % 2) * D_MODEL
                    v.wait_ge(sem_s, j + 1)
                    if j >= 2:
                        v.wait_ge(sem_son[(j - 2) % 2],
                                  16 * ((j - 2) // 2 + 1))
                    xtj = x2t[:, parj:parj + D_MODEL]
                    c = lambda nm: scr[nm][:, j:j + 1]
                    v.tensor_tensor(c("t3"), c("t2"), c("t2"), ALU.mult)
                    v.tensor_tensor(c("t3"), c("t3"), c("var"), ALU.mult)
                    v.tensor_scalar(c("t3"), c("t3"), -0.5, 1.5, ALU.mult,
                                    ALU.add)
                    v.tensor_tensor(c("t2"), c("t2"), c("t3"), ALU.mult)
                    v.tensor_scalar(xn2t[:, parj:parj + D_MODEL], xtj,
                                    c("mean"), c("t2"), ALU.subtract, ALU.mult)
                    v.tensor_reduce(amx[:, j:j + 1],
                                    xn2t[:, parj:parj + D_MODEL],
                                    AXX, ALU.max, apply_absolute_value=True)
                    V.mark(v)
                    vB[j] = V.n
            v.tensor_reduce(amx[:, 16:17], amx[:, 0:16], AXX, ALU.max)
            V.mark(v)
        v_fin = V.n

        @blk.tensor
        def _(t):
            t.wait_ge(sem_ld, 2 * 16)
            t.wait_ge(sem_v, 1)
            for i in range(16):
                pw = psw[i % 3]
                if i >= 3:
                    t.wait_ge(sem_v, vAdd[i - 3])
                if i == 1:
                    t.wait_ge(sem_v, 17)
                for nt in range(2):
                    for kc in range(KC2):
                        if i == 0 and nt == 0:
                            t.wait_ge(sem_y[kc % 4], 16 * (kc // 4 + 1))
                            t.wait_ge(sem_v, kc + 2)
                            t.wait_ge(sem_wo[kc // 2], 16)
                        t.matmul(pw[:, nt * 512:nt * 512 + 512],
                                 ofe[:, kc * T + i * 128: kc * T + (i + 1) * 128],
                                 wob[:, kc * D_MODEL + nt * 512:
                                     kc * D_MODEL + nt * 512 + 512],
                                 start=(kc == 0), stop=False)
                    t.matmul(pw[:, nt * 512:nt * 512 + 512],
                             onesr[0:1, 0:128],
                             bor[0:1, nt * 512:nt * 512 + 512],
                             start=False, stop=True)
                t.drain().then_inc(sem_t, 1)
                TT.n += 1
                assert tTile[i] == TT.n
            t.wait_ge(sem_v, v_fin)
            t.transpose(pst[0:1, 0:128], amx[:, 16:17], identf[:])
            t.drain().then_inc(sem_t, 1)
            TT.n += 1
        t_fin = TT.n

        @blk.scalar
        def _(s):
            s = eps.bind(s)
            for i in range(16):
                s.wait_ge(sem_v, vA[i])
                s.activation(scr["t2"][:, i:i + 1], scr["t1"][:, i:i + 1],
                             AF.Sqrt)
                SS.mark(s)

        @blk.vector
        def _(v):
            v = epv.bind(v)
            v.wait_ge(sem_t, t_fin)
            v.tensor_copy(amr[:], pst[0:1, 0:128])
            v.tensor_reduce(amo[:], amr[:], AXX, ALU.max)
            V.mark(v)
        v_amo = V.n

        @blk.gpsimd
        def _(g):
            g.wait_ge(sem_v, v_amo)
            g.dma_start(o_amax.ap(), amo[:]).then_inc(sem_md, 16)
            g.wait_ge(sem_md, 16)
            for q in range(2):
                g.wait_ge(sem_sox[q], 16 * 8)
                g.wait_ge(sem_son[q], 16 * 8)
        blk.sync(_sync_body)
        bstk.close()
    return nc


def build_stage4():
    nc = bass.Bass("TRN2", target_bir_lowering=False)
    OC, NT, TS, KC = 32, 4, 512, 8
    xn2 = nc.dram_tensor("xn2", [T, D_MODEL], F32, kind="ExternalInput")
    amax8 = nc.dram_tensor("amax8", [1, 8], F32, kind="ExternalInput")
    w_inT = nc.dram_tensor("w_inT", [4096, D_MODEL], F32, kind="ExternalInput")
    bin_c = nc.dram_tensor("bin_c", [128, 32], F32, kind="ExternalInput")
    identw = nc.dram_tensor("identw", [128, 128], F32, kind="ExternalInput")
    identwb = nc.dram_tensor("identwb", [128, 128], BF16, kind="ExternalInput")
    o_hid = nc.dram_tensor("o_hid", [4096, T], BF16, kind="ExternalOutput")
    o_amax = nc.dram_tensor("o_amax", [1, 1], F32, kind="ExternalOutput")

    with ExitStack() as ctx:
        e = ctx.enter_context
        kq = e(nc.sbuf_tensor("kq", [128, KC * 4096], BF16))
        rxf = e(nc.sbuf_tensor("rxf", [128, KC * T], BF16))
        hidb = e(nc.sbuf_tensor("hidb", [128, 4 * T], BF16))
        wqt = e(nc.sbuf_tensor("wqt", [128, 2 * D_MODEL], F32))
        kqt = e(nc.sbuf_tensor("kqt", [128, 2 * D_MODEL], BF16))
        xtt = e(nc.sbuf_tensor("xtt", [128, 2 * D_MODEL], F32))
        rxt = e(nc.sbuf_tensor("rxt", [128, 2 * D_MODEL], BF16))
        colsc = e(nc.sbuf_tensor("colsc", [128, 32], F32))
        binv = e(nc.sbuf_tensor("binv", [128, 32], F32))
        am8 = e(nc.sbuf_tensor("am8", [1, 8], F32))
        sx1 = e(nc.sbuf_tensor("sx1", [1, 2], F32))
        sxc = e(nc.sbuf_tensor("sxc", [128, 2], F32))
        onesr = e(nc.sbuf_tensor("onesr", [1, 128], F32))
        identf = e(nc.sbuf_tensor("identf", [128, 128], F32))
        identb = e(nc.sbuf_tensor("identb", [128, 128], BF16))
        amx = e(nc.sbuf_tensor("amx", [128, 33], F32))
        amr = e(nc.sbuf_tensor("amr", [1, 128], F32))
        amo = e(nc.sbuf_tensor("amo", [1, 1], F32))
        wsc = e(nc.sbuf_tensor("wsc", [128, 3 * 32], F32))
        ps = [e(nc.psum_tensor(f"ps{i}", [128, TS], F32)) for i in range(4)]
        pst = [e(nc.psum_tensor(f"pst{i}", [128, 1024], BF16)) for i in range(2)]
        psb = e(nc.psum_tensor("psb", [128, 512], F32))
        sem_ld = e(nc.semaphore("s_ld"))
        sem_w0 = e(nc.semaphore("s_w0"))
        sem_w1 = e(nc.semaphore("s_w1"))
        sem_x0 = e(nc.semaphore("s_x0"))
        sem_x1 = e(nc.semaphore("s_x1"))
        sem_v = e(nc.semaphore("s_v"))
        sem_s = e(nc.semaphore("s_s"))
        sem_t = e(nc.semaphore("s_t"))
        sem_so = [e(nc.semaphore(f"s_so{i}")) for i in range(4)]
        sem_fin = e(nc.semaphore("s_fin"))
        LD = Ctr(sem_ld, 16)
        WL = [Ctr(sem_w0, 16), Ctr(sem_w1, 16)]
        XL = [Ctr(sem_x0, 16), Ctr(sem_x1, 16)]
        V = Ctr(sem_v)
        SS = Ctr(sem_s)
        TT = Ctr(sem_t)
        SO = Ctr(sem_so, 16)
        epv, eps = EP(), EP()

        bstk = ExitStack()
        blk = bstk.enter_context(nc.Block())

        @blk.sync
        def _(sync):
            LD.inc(sync.dma_start(am8[:], amax8.ap()))
            LD.inc(sync.dma_start(binv[:], bin_c.ap()))
            LD.inc(sync.dma_start(identf[:], identw.ap()))
            LD.inc(sync.dma_start(identb[:], identwb.ap()))
        ld_all = LD.n

        @blk.vector
        def _(v):
            v = epv.bind(v)
            v.memset(onesr[:], 1.0)
            v.wait_ge(sem_ld, ld_all)
            v.tensor_reduce(sx1[0:1, 1:2], am8[:], AXX, ALU.max)
            v.reciprocal(sx1[0:1, 0:1], sx1[0:1, 1:2])
            v.tensor_scalar(sx1[0:1, 0:1], sx1[0:1, 0:1], 32767.0, None, ALU.mult)
            v.tensor_scalar(sx1[0:1, 1:2], sx1[0:1, 1:2], 1.0 / 32767.0, None,
                            ALU.mult)
            V.mark(v)
        v_sx = V.n

        @blk.tensor
        def _(t):
            t.wait_ge(sem_v, v_sx)
            t.matmul(psb[:, 0:2], onesr[:], sx1[0:1, 0:2], start=True, stop=True)
            t.drain().then_inc(sem_t, 1)
            TT.n += 1
        t_sx = TT.n

        @blk.scalar
        def _(s):
            s = eps.bind(s)
            s.wait_ge(sem_t, t_sx)
            SS.inc(s.activation(sxc[:], psb[:, 0:2], AF.Copy))
        s_sx = SS.n

        @blk.gpsimd
        def _(g):
            for m in range(32):
                buf = m % 2
                if m >= 2:
                    g.wait_ge(sem_v, v_sx + (m - 1))
                WL[buf].inc(g.dma_start(
                    wqt[:, buf * D_MODEL:(buf + 1) * D_MODEL],
                    w_inT[m * 128:(m + 1) * 128, :]))

        @blk.vector
        def _(v):
            v = epv.bind(v)
            v.wait_ge(sem_s, s_sx)
            for m in range(32):
                buf = m % 2
                v.wait_ge([sem_w0, sem_w1][buf], 16 * (m // 2 + 1))
                if m >= 2:
                    v.wait_ge(sem_t, t_sx + (m - 1) * KC)
                wt = wqt[:, buf * D_MODEL:(buf + 1) * D_MODEL]
                v.tensor_reduce(amx[:, 32:33], wt, AXX, ALU.max,
                                apply_absolute_value=True)
                v.tensor_scalar_max(amx[:, 32:33], amx[:, 32:33], 1e-8)
                v.reciprocal(wsc[:, 64 + 0:64 + 1], amx[:, 32:33])
                v.tensor_scalar(wsc[:, 64:65], wsc[:, 64:65], 127.0, None,
                                ALU.mult)
                v.tensor_scalar(colsc[:, m:m + 1], amx[:, 32:33],
                                1.0 / 127.0, None, ALU.mult)
                v.tensor_tensor(colsc[:, m:m + 1], colsc[:, m:m + 1],
                                sxc[:, 1:2], ALU.mult)
                v.tensor_scalar(wt, wt, wsc[:, 64:65], MAGIC, ALU.mult, ALU.add)
                v.tensor_scalar(kqt[:, buf * D_MODEL:(buf + 1) * D_MODEL],
                                wt, -MAGIC, None, ALU.add)
                V.mark(v)
        v_wq = V.n

        @blk.tensor
        def _(t):
            for m in range(32):
                buf = m % 2
                t.wait_ge(sem_v, v_sx + m + 1)
                for kc in range(KC):
                    n = m * KC + kc
                    if n >= 2:
                        t.wait_ge(sem_s, s_sx + (n - 1))
                    t.transpose(pst[n % 2][:, 0:128],
                                kqt[:, buf * D_MODEL + kc * 128:
                                    buf * D_MODEL + (kc + 1) * 128],
                                identb[:])
                    t.drain().then_inc(sem_t, 1)
                    TT.n += 1
        t_tr = TT.n

        @blk.scalar
        def _(s):
            s = eps.bind(s)
            for m in range(32):
                for kc in range(KC):
                    n = m * KC + kc
                    s.wait_ge(sem_t, t_sx + n + 1)
                    SS.inc(s.activation(
                        kq[:, kc * 4096 + m * 128: kc * 4096 + (m + 1) * 128],
                        pst[n % 2][:, 0:128], AF.Copy))
        s_kq = SS.n
        bstk.close()

        bstk2 = ExitStack()
        blk2 = bstk2.enter_context(nc.Block())

        @blk2.sync
        def _(sync):
            for i in range(16):
                if i >= 2:
                    sync.wait_ge(sem_v, v_wq + (i - 1))
                XL[i % 2].inc(sync.dma_start(
                    xtt[:, (i % 2) * D_MODEL:(i % 2 + 1) * D_MODEL],
                    xn2[i * 128:(i + 1) * 128, :]))

        @blk2.vector
        def _(v):
            v = epv.bind(v)
            for i in range(16):
                buf = i % 2
                v.wait_ge([sem_x0, sem_x1][buf], 16 * (i // 2 + 1))
                if i >= 2:
                    v.wait_ge(sem_t, t_tr + (i - 1) * KC)
                xti = xtt[:, buf * D_MODEL:(buf + 1) * D_MODEL]
                v.tensor_scalar(xti, xti, sxc[:, 0:1], MAGIC, ALU.mult, ALU.add)
                v.tensor_scalar(rxt[:, buf * D_MODEL:(buf + 1) * D_MODEL],
                                xti, -MAGIC, None, ALU.add)
                V.mark(v)
        v_rx = V.n

        @blk2.tensor
        def _(t):
            for i in range(16):
                buf = i % 2
                t.wait_ge(sem_v, v_wq + i + 1)
                for kc in range(KC):
                    n = i * KC + kc
                    if n >= 2:
                        t.wait_ge(sem_s, s_kq + (n - 1))
                    t.transpose(pst[n % 2][:, 0:128],
                                rxt[:, buf * D_MODEL + kc * 128:
                                    buf * D_MODEL + (kc + 1) * 128],
                                identb[:])
                    t.drain().then_inc(sem_t, 1)
                    TT.n += 1
        t_rx = TT.n

        @blk2.scalar
        def _(s):
            s = eps.bind(s)
            for i in range(16):
                for kc in range(KC):
                    n = i * KC + kc
                    s.wait_ge(sem_t, t_tr + n + 1)
                    SS.inc(s.activation(
                        rxf[:, kc * T + i * 128: kc * T + (i + 1) * 128],
                        pst[n % 2][:, 0:128], AF.Copy))
        s_rx = SS.n
        bstk2.close()

        bstk3 = ExitStack()
        blk3 = bstk3.enter_context(nc.Block())

        @blk3.tensor
        def _(t):
            t.wait_ge(sem_s, s_rx)
            for oc in range(OC):
                for ts in range(NT):
                    g_i = oc * NT + ts
                    if g_i >= 4:
                        t.wait_ge(sem_s, s_rx + (g_i - 4) + 1)
                    for kc in range(KC):
                        t.matmul(ps[g_i % 4][:],
                                 kq[:, kc * 4096 + oc * 128:
                                     kc * 4096 + (oc + 1) * 128],
                                 rxf[:, kc * T + ts * TS: kc * T + ts * TS + TS],
                                 start=(kc == 0), stop=(kc == KC - 1))
                    t.drain().then_inc(sem_t, 1)
                    TT.n += 1
        t_mm = TT.n

        @blk3.scalar
        def _(s):
            s = eps.bind(s)
            for g_i in range(OC * NT):
                oc, ts = divmod(g_i, NT)
                s.wait_ge(sem_t, t_rx + g_i + 1)
                if oc >= 4 and ts == 0:
                    s.wait_ge(sem_so[oc % 4], 16 * (oc // 4))
                SS.inc(s.activation(
                    hidb[:, (oc % 4) * T + ts * TS:(oc % 4) * T + ts * TS + TS],
                    ps[g_i % 4][:], AF.Gelu,
                    bias=binv[:, oc:oc + 1], scale=colsc[:, oc:oc + 1]))
        s_gl = SS.n

        @blk3.vector
        def _(v):
            v = epv.bind(v)
            for oc in range(OC):
                v.wait_ge(sem_s, s_rx + oc * NT + NT)
                v.tensor_reduce(amx[:, oc:oc + 1],
                                hidb[:, (oc % 4) * T:(oc % 4) * T + T],
                                AXX, ALU.max, apply_absolute_value=True)
                V.mark(v)
            v.tensor_reduce(amx[:, 32:33], amx[:, 0:32], AXX, ALU.max)
            V.mark(v)
        v_am = V.n

        @blk3.sync
        def _(sync):
            for oc in range(OC):
                sync.wait_ge(sem_v, v_rx + oc + 1)
                sync.dma_start(o_hid[oc * 128:(oc + 1) * 128, :],
                               hidb[:, (oc % 4) * T:(oc % 4) * T + T]
                               ).then_inc(sem_so[oc % 4], 16)
                SO.n += 16

        @blk3.tensor
        def _(t):
            t.wait_ge(sem_v, v_am)
            t.transpose(psb[0:1, 0:128], amx[:, 32:33], identf[:])
            t.drain().then_inc(sem_t, 1)
            TT.n += 1
        t_am = TT.n

        @blk3.vector
        def _(v):
            v = epv.bind(v)
            v.wait_ge(sem_t, t_am)
            v.tensor_copy(amr[:], psb[0:1, 0:128])
            v.tensor_reduce(amo[:], amr[:], AXX, ALU.max)
            V.mark(v)
        v_fin = V.n

        @blk3.gpsimd
        def _(g):
            g.wait_ge(sem_v, v_fin)
            g.dma_start(o_amax.ap(), amo[:]).then_inc(sem_fin, 16)
            g.wait_ge(sem_fin, 16)
            for q in range(4):
                g.wait_ge(sem_so[q], 16 * (OC // 4))
        bstk3.close()
    return nc


def build_stage5():
    nc = bass.Bass("TRN2", target_bir_lowering=False)
    OC, TS, KC = 8, 512, 32
    TH = T // 2
    hid = nc.dram_tensor("hid", [4096, T], BF16, kind="ExternalInput")
    amax8 = nc.dram_tensor("amax8", [1, 8], F32, kind="ExternalInput")
    w_out = nc.dram_tensor("w_out", [4096, D_MODEL], F32, kind="ExternalInput")
    bout_c = nc.dram_tensor("bout_c", [128, 8], F32, kind="ExternalInput")
    x2 = nc.dram_tensor("x2", [T, D_MODEL], F32, kind="ExternalInput")
    identw = nc.dram_tensor("identw", [128, 128], F32, kind="ExternalInput")
    o_out = nc.dram_tensor("o_out", [T, D_MODEL], F32, kind="ExternalOutput")
    ffnd = nc.dram_tensor("ffnd", [D_MODEL, T], BF16, kind="Internal")

    with ExitStack() as ctx:
        e = ctx.enter_context
        wqb = e(nc.sbuf_tensor("wqb", [128, KC * D_MODEL], BF16))
        rhf = e(nc.sbuf_tensor("rhf", [128, KC * TH], BF16))
        wsb = e(nc.sbuf_tensor("wsb", [128, 2 * D_MODEL], F32))
        tmpf = e(nc.sbuf_tensor("tmpf", [128, 2 * TH], F32))
        hbb = e(nc.sbuf_tensor("hbb", [128, 2 * TH], BF16))
        ffnf = e(nc.sbuf_tensor("ffnf", [128, 2 * TH], BF16))
        bov = e(nc.sbuf_tensor("bov", [128, 8], F32))
        am8 = e(nc.sbuf_tensor("am8", [1, 8], F32))
        sx1 = e(nc.sbuf_tensor("sx1", [1, 2], F32))
        sxc = e(nc.sbuf_tensor("sxc", [128, 2], F32))
        onesr = e(nc.sbuf_tensor("onesr", [1, 128], F32))
        wamx = e(nc.sbuf_tensor("wamx", [128, 2], F32))
        xt = e(nc.sbuf_tensor("xt", [128, 2 * D_MODEL], F32))
        fxt = e(nc.sbuf_tensor("fxt", [128, 2 * D_MODEL], BF16))
        ot = e(nc.sbuf_tensor("ot", [128, 2 * D_MODEL], F32))
        ps = [e(nc.psum_tensor(f"ps{i}", [128, TS], F32)) for i in range(4)]
        psb = e(nc.psum_tensor("psb", [128, 512], F32))
        sem_ld = e(nc.semaphore("s_ld"))
        sem_w0 = e(nc.semaphore("s_w0"))
        sem_w1 = e(nc.semaphore("s_w1"))
        sem_h0 = e(nc.semaphore("s_h0"))
        sem_h1 = e(nc.semaphore("s_h1"))
        sem_v = e(nc.semaphore("s_v"))
        sem_s = e(nc.semaphore("s_s"))
        sem_t = e(nc.semaphore("s_t"))
        sem_md = e(nc.semaphore("s_md"))
        sem_ml = [e(nc.semaphore(f"s_ml{i}")) for i in range(3)]
        sem_x = [e(nc.semaphore(f"s_x{i}")) for i in range(3)]
        sem_so = [e(nc.semaphore(f"s_so{i}")) for i in range(2)]
        LD = Ctr(sem_ld, 16)
        V = Ctr(sem_v)
        SS = Ctr(sem_s)
        TT = Ctr(sem_t)
        MD = Ctr(sem_md, 16)
        epv, eps = EP(), EP()

        bstk = ExitStack()
        blk = bstk.enter_context(nc.Block())

        @blk.sync
        def _(sync):
            LD.inc(sync.dma_start(am8[:], amax8.ap()))
            LD.inc(sync.dma_start(bov[:], bout_c.ap()))
        ld_all = LD.n

        @blk.vector
        def _(v):
            v = epv.bind(v)
            v.memset(onesr[:], 1.0)
            v.wait_ge(sem_ld, ld_all)
            v.tensor_reduce(sx1[0:1, 1:2], am8[:], AXX, ALU.max)
            v.reciprocal(sx1[0:1, 0:1], sx1[0:1, 1:2])
            v.tensor_scalar(sx1[0:1, 0:1], sx1[0:1, 0:1], 32767.0, None, ALU.mult)
            v.tensor_scalar(sx1[0:1, 1:2], sx1[0:1, 1:2], 1.0 / 32767.0, None,
                            ALU.mult)
            V.mark(v)
        v_sx = V.n

        @blk.tensor
        def _(t):
            t.wait_ge(sem_v, v_sx)
            t.matmul(psb[:, 0:2], onesr[:], sx1[0:1, 0:2], start=True, stop=True)
            t.drain().then_inc(sem_t, 1)
            TT.n += 1
        t_sx = TT.n

        @blk.scalar
        def _(s):
            s = eps.bind(s)
            s.wait_ge(sem_t, t_sx)
            SS.inc(s.activation(sxc[:], psb[:, 0:2], AF.Copy))
        s_sx = SS.n

        @blk.gpsimd
        def _(g):
            for kc in range(KC):
                buf = kc % 2
                if kc >= 2:
                    g.wait_ge(sem_v, v_sx + (kc - 1))
                g.dma_start(
                    wsb[:, buf * D_MODEL: buf * D_MODEL + D_MODEL],
                    w_out[kc * 128:(kc + 1) * 128, :]
                ).then_inc([sem_w0, sem_w1][buf], 16)

        @blk.vector
        def _(v):
            v = epv.bind(v)
            for kc in range(KC):
                buf = kc % 2
                v.wait_ge([sem_w0, sem_w1][buf], 16 * (kc // 2 + 1))
                wt = wsb[:, buf * D_MODEL: buf * D_MODEL + D_MODEL]
                v.tensor_reduce(wamx[:, 0:1], wt, AXX, ALU.max,
                                apply_absolute_value=True)
                v.tensor_scalar_max(wamx[:, 0:1], wamx[:, 0:1], 1e-8)
                v.reciprocal(wamx[:, 1:2], wamx[:, 0:1])
                v.tensor_scalar(wamx[:, 1:2], wamx[:, 1:2], 127.0, None, ALU.mult)
                v.tensor_scalar(wamx[:, 0:1], wamx[:, 0:1], 1.0 / 127.0, None,
                                ALU.mult)
                v.tensor_scalar(wt, wt, wamx[:, 1:2], MAGIC, ALU.mult, ALU.add)
                v.tensor_scalar(wt, wt, -MAGIC, None, ALU.add)
                v.tensor_scalar(wqb[:, kc * D_MODEL:(kc + 1) * D_MODEL],
                                wt, wamx[:, 0:1], None, ALU.mult)
                V.mark(v)
        v_wq = V.n

        @blk.sync
        def _(sync):
            for th in range(2):
                for kc in range(KC):
                    n = th * KC + kc
                    if n >= 2:
                        sync.wait_ge(sem_v, v_wq + (n - 1))
                    sync.dma_start(
                        hbb[:, (n % 2) * TH:(n % 2 + 1) * TH],
                        hid[kc * 128:(kc + 1) * 128, th * TH:(th + 1) * TH]
                    ).then_inc([sem_h0, sem_h1][n % 2], 16)

        @blk.vector
        def _(v):
            v = epv.bind(v)
            v.wait_ge(sem_s, s_sx)
            for th in range(2):
                for kc in range(KC):
                    n = th * KC + kc
                    buf = n % 2
                    v.wait_ge([sem_h0, sem_h1][buf], 16 * (n // 2 + 1))
                    if n >= 2:
                        if th == 1:
                            v.wait_ge(sem_t, t_sx + 1 * (OC * 2))
                    hin = hbb[:, buf * TH:(buf + 1) * TH]
                    v.tensor_scalar(tmpf[:, buf * TH:(buf + 1) * TH], hin,
                                    sxc[:, 0:1], MAGIC, ALU.mult, ALU.add)
                    v.tensor_scalar(rhf[:, kc * TH:(kc + 1) * TH],
                                    tmpf[:, buf * TH:(buf + 1) * TH],
                                    -MAGIC, None, ALU.add)
                    V.mark(v)
        v_rh = V.n

        @blk.tensor
        def _(t):
            for th in range(2):
                t.wait_ge(sem_v, v_wq + (th + 1) * KC)
                for oc in range(OC):
                    for ts in range(2):
                        g_i = th * OC * 2 + oc * 2 + ts
                        if g_i >= 4:
                            t.wait_ge(sem_s, s_sx + (g_i - 4) + 1)
                        for kc in range(KC):
                            t.matmul(ps[g_i % 4][:],
                                     wqb[:, kc * D_MODEL + oc * 128:
                                         kc * D_MODEL + (oc + 1) * 128],
                                     rhf[:, kc * TH + ts * TS:
                                         kc * TH + ts * TS + TS],
                                     start=(kc == 0), stop=(kc == KC - 1))
                        t.drain().then_inc(sem_t, 1)
                        TT.n += 1
        t_mm = TT.n

        @blk.scalar
        def _(s):
            s = eps.bind(s)
            for g_i in range(2 * OC * 2):
                th, rem = divmod(g_i, OC * 2)
                oc, ts = divmod(rem, 2)
                s.wait_ge(sem_t, t_sx + g_i + 1)
                if th * 8 + oc >= 2 and ts == 0:
                    s.wait_ge(sem_md, 16 * (th * 8 + oc))
                SS.inc(s.activation(
                    ffnf[:, (oc % 2) * TH + ts * TS:
                         (oc % 2) * TH + ts * TS + TS],
                    ps[g_i % 4][:], AF.Identity,
                    bias=bov[:, oc:oc + 1], scale=sxc[:, 1:2]))
        s_mm = SS.n

        @blk.gpsimd
        def _(g):
            for th in range(2):
                for oc in range(OC):
                    done = s_sx + th * OC * 2 + oc * 2 + 2
                    g.wait_ge(sem_s, done)
                    MD.inc(g.dma_start(
                        ffnd[oc * 128:(oc + 1) * 128, th * TH:(th + 1) * TH],
                        ffnf[:, (oc % 2) * TH:(oc % 2) * TH + TH]))
        md_all = MD.n
        bstk.close()

        bstk2 = ExitStack()
        blk2 = bstk2.enter_context(nc.Block())

        @blk2.sync
        def _(sync):
            sync.wait_ge(sem_md, md_all)
            for i in range(16):
                if i >= 2:
                    sync.wait_ge(sem_v, v_rh + (i - 1))
                sync.dma_start(
                    fxt[:, (i % 2) * D_MODEL:(i % 2 + 1) * D_MODEL],
                    ffnd[:, i * 128:(i + 1) * 128],
                    transpose=True).then_inc(sem_ml[i % 3], 16)
                sync.dma_start(
                    xt[:, (i % 2) * D_MODEL:(i % 2 + 1) * D_MODEL],
                    x2[i * 128:(i + 1) * 128, :]).then_inc(sem_x[i % 3], 16)
                if i >= 2:
                    sync.wait_ge(sem_v, v_rh + (i - 1))
                    sync.dma_start(
                        o_out[(i - 2) * 128:(i - 1) * 128, :],
                        ot[:, ((i - 2) % 2) * D_MODEL:((i - 2) % 2 + 1) * D_MODEL]
                    ).then_inc(sem_so[(i - 2) % 2], 16)
            for i in (14, 15):
                sync.wait_ge(sem_v, v_rh + i + 1)
                sync.dma_start(
                    o_out[i * 128:(i + 1) * 128, :],
                    ot[:, (i % 2) * D_MODEL:(i % 2 + 1) * D_MODEL]
                ).then_inc(sem_so[i % 2], 16)

        @blk2.vector
        def _(v):
            v = epv.bind(v)
            for i in range(16):
                par = (i % 2) * D_MODEL
                v.wait_ge(sem_ml[i % 3], 16 * (i // 3 + 1))
                v.wait_ge(sem_x[i % 3], 16 * (i // 3 + 1))
                if i >= 2:
                    v.wait_ge(sem_so[(i - 2) % 2], 16 * ((i - 2) // 2 + 1))
                v.tensor_tensor(ot[:, par:par + D_MODEL], xt[:, par:par + D_MODEL],
                                fxt[:, par:par + D_MODEL], ALU.add)
                V.mark(v)
        v_fin = V.n

        @blk2.gpsimd
        def _(g):
            g.wait_ge(sem_so[0], 16 * 8)
            g.wait_ge(sem_so[1], 16 * 8)
        bstk2.close()
    return nc


def kernel(**inputs):
    x = np.ascontiguousarray(inputs["L0_attn_x"]).astype(np.float32, copy=False)
    wconv = np.ascontiguousarray(
        inputs["L0_attn_conv_w"].transpose(2, 1, 0).reshape(3 * D_MODEL, D_INNER)
    ).astype(np.float32, copy=False)
    wz = np.ascontiguousarray(inputs["L0_attn_in_proj_w"][:, D_INNER:]).astype(
        np.float32, copy=False)
    wg = np.ascontiguousarray(inputs["L0_attn_gate_w"]).astype(np.float32, copy=False)
    wxp = np.ascontiguousarray(inputs["L0_attn_x_proj_w"]).astype(np.float32, copy=False)
    wo = np.ascontiguousarray(inputs["L0_attn_o_proj_w"]).astype(np.float32, copy=False)
    wo_t = np.ascontiguousarray(
        wo.reshape(16, 128, 1024).transpose(1, 0, 2).reshape(128, 16384))
    bo_r = np.ascontiguousarray(inputs["L0_attn_o_proj_b"][None, :]).astype(np.float32, copy=False)
    w_inT = np.ascontiguousarray(inputs["L0_ffn_W_in"].T).astype(np.float32, copy=False)
    w_out = np.ascontiguousarray(inputs["L0_ffn_W_out"]).astype(np.float32, copy=False)

    biases1 = np.zeros((128, 49), np.float32)
    biases1[:, 0:16] = inputs["L0_attn_conv_b"].reshape(16, 128).T
    biases1[:, 16:32] = inputs["L0_attn_gate_b"].reshape(16, 128).T
    biases1[:, 32:48] = inputs["L0_attn_in_proj_b"][D_INNER:].reshape(16, 128).T
    biases1[0:32, 48] = inputs["L0_attn_x_proj_b"]
    bo_c = np.zeros((128, 8), np.float32)
    bo_c[:, :] = inputs["L0_attn_o_proj_b"].reshape(8, 128).T
    bin_c = np.zeros((128, 32), np.float32)
    bin_c[:, :] = inputs["L0_ffn_b_in"].reshape(32, 128).T
    bout_c = np.zeros((128, 8), np.float32)
    bout_c[:, :] = inputs["L0_ffn_b_out"].reshape(8, 128).T

    identw = np.eye(128, dtype=np.float32)
    identwb = np.eye(128, dtype=BFNP)
    tri = np.zeros((128, 128), np.float32)
    for k in range(127):
        tri[k, k:] = 1.0
    tri[127, :] = 1.0
    tri[:, 127] = 1.0
    tri = tri.astype(BFNP)
    zrowd = np.zeros((1, 16 * 1024), BFNP)
    onesd = np.ones((1, 2048), BFNP)

    nc1 = build_stage1()
    maps1 = []
    for c in range(8):
        b, h = c // 2, c % 2
        t0 = h * T
        if t0 == 0:
            xh = np.concatenate([np.zeros((2, D_MODEL), np.float32), x[b, 0:T]], 0)
        else:
            xh = np.ascontiguousarray(x[b, t0 - 2:t0 + T])
        maps1.append({"xh": xh, "wconv": wconv, "wz": wz, "wg": wg, "wxp": wxp,
                      "biases": biases1, "identw": identw})
    r1 = run_bass_kernel_spmd(nc1, maps1, core_ids=list(range(8))).results

    nc2 = build_stage2()
    maps2 = []
    A_log = np.ascontiguousarray(inputs["L0_attn_A_log"]).astype(np.float32, copy=False)
    Dv = np.ascontiguousarray(inputs["L0_attn_D"]).astype(np.float32, copy=False)
    for c in range(8):
        b, j = c // 2, c % 2
        upad = np.zeros((1024, S + 128), BFNP)
        upad[:, 0:T] = r1[2 * b]["o_u"][j * 1024:(j + 1) * 1024, :]
        upad[:, T:S] = r1[2 * b + 1]["o_u"][j * 1024:(j + 1) * 1024, :]
        bcdt = np.concatenate([r1[2 * b]["o_bcdt"], r1[2 * b + 1]["o_bcdt"]], 1)
        maps2.append({"u_d": upad,
                      "bcdt": np.ascontiguousarray(bcdt),
                      "a_half": np.ascontiguousarray(A_log[j * 1024:(j + 1) * 1024]),
                      "d_half": np.ascontiguousarray(Dv[None, j * 1024:(j + 1) * 1024]),
                      "ltri_c": tri, "zrowd": zrowd, "onesd": onesd,
                      "identw": identw})
    r2 = run_bass_kernel_spmd(nc2, maps2, core_ids=list(range(8))).results

    nc3 = build_stage3()
    maps3 = []
    for c in range(8):
        b, h = c // 2, c % 2
        t0 = h * T
        y_own = np.concatenate([r2[2 * b]["o_y"][t0:t0 + T, :],
                                r2[2 * b + 1]["o_y"][t0:t0 + T, :]], 1)
        maps3.append({"y_own": np.ascontiguousarray(y_own),
                      "sigz": r1[c]["o_sigz"],
                      "x_own": np.ascontiguousarray(x[b, t0:t0 + T]),
                      "wo_t": wo_t, "bo_r": bo_r, "identw": identw})
    r3 = run_bass_kernel_spmd(nc3, maps3, core_ids=list(range(8))).results

    nc4 = build_stage4()
    amax8 = np.concatenate([r3[c]["o_amax"] for c in range(8)], 1)
    maps4 = [{"xn2": r3[c]["o_xn2"], "amax8": amax8, "w_inT": w_inT,
              "bin_c": bin_c, "identw": identw, "identwb": identwb}
             for c in range(8)]
    r4 = run_bass_kernel_spmd(nc4, maps4, core_ids=list(range(8))).results

    nc5 = build_stage5()
    amaxh8 = np.concatenate([r4[c]["o_amax"] for c in range(8)], 1)
    maps5 = [{"hid": r4[c]["o_hid"], "amax8": amaxh8, "w_out": w_out,
              "bout_c": bout_c, "x2": r3[c]["o_x2"], "identw": identw}
             for c in range(8)]
    r5 = run_bass_kernel_spmd(nc5, maps5, core_ids=list(range(8))).results

    out = np.zeros((B, S, D_MODEL), np.float32)
    for c in range(8):
        b, h = c // 2, c % 2
        out[b, h * T:(h + 1) * T] = r5[c]["o_out"]
    return out



# version 13 (final)
# speedup vs baseline: 1.0204x; 1.0204x over previous
import numpy as np
import ml_dtypes
from contextlib import ExitStack
import concourse.bass as bass
import concourse.mybir as mybir
from concourse.bass_utils import run_bass_kernel_spmd

F32 = mybir.dt.float32
BF16 = mybir.dt.bfloat16
AF = mybir.ActivationFunctionType
ALU = mybir.AluOpType
AXX = mybir.AxisListType.X
BFNP = ml_dtypes.bfloat16

D_MODEL = 1024
D_INNER = 2048
B = 4
S = 4096
T = 2048
MAGIC = 12582912.0
LN_EPS = 1e-5




def _ap_names(args, kwargs):
    names = []
    for a in list(args) + list(kwargs.values()):
        t = getattr(a, "tensor", None)
        if t is not None:
            names.append(t.name)
    return names


class EP:

    _OPS = ("tensor_copy", "tensor_tensor", "tensor_add", "tensor_sub",
            "tensor_mul", "tensor_max", "tensor_scalar", "tensor_single_scalar",
            "tensor_scalar_add", "tensor_scalar_sub", "tensor_scalar_mul",
            "tensor_scalar_max", "tensor_scalar_min", "tensor_relu",
            "tensor_reduce", "reduce_max", "reduce_sum", "scalar_tensor_tensor",
            "tensor_tensor_scan", "reciprocal", "select",
            "activation", "memset", "copy", "mul", "add", "iota")

    def __init__(self):
        self._e = None
        self._written = set()
        self._read = set()

    def bind(self, eng):
        self._e = eng
        return self

    def __getattr__(self, name):
        if name.startswith("_") or name in ("bind", "barrier"):
            raise AttributeError(name)
        f = getattr(self._e, name)
        if name not in EP._OPS:
            return f

        def wrap(*args, **kwargs):
            names = _ap_names(args, kwargs)
            out = names[0] if names else None
            ins = names[1:]
            if (out in self._written or out in self._read
                    or any(n in self._written for n in ins)):
                self._e.drain()
                self._written.clear()
                self._read.clear()
            r = f(*args, **kwargs)
            if out is not None:
                self._written.add(out)
            self._read.update(ins)
            return r
        return wrap

    def barrier(self):
        self._e.drain()
        self._written.clear()
        self._read.clear()


class Ctr:

    def __init__(self, sem, step=1):
        self.sem = sem
        self.n = 0
        self.step = step

    def inc(self, inst):
        self.n += self.step
        return inst.then_inc(self.sem, self.step)

    def ginc(self, eng, inst, depth=6):
        if self.n >= depth * self.step:
            eng.wait_ge(self.sem, self.n - (depth - 1) * self.step)
        return self.inc(inst)

    def mark(self, eng):
        ep = eng if not isinstance(eng, EP) else eng._e
        self.n += self.step
        return ep.drain().then_inc(self.sem, self.step)


def build_stage1():
    nc = bass.Bass("TRN2", target_bir_lowering=False)
    KC, OC, NT, TS = 8, 16, 4, 512
    xh = nc.dram_tensor("xh", [T + 2, D_MODEL], F32, kind="ExternalInput")
    wconv = nc.dram_tensor("wconv", [3 * D_MODEL, D_INNER], F32, kind="ExternalInput")
    wz = nc.dram_tensor("wz", [D_MODEL, D_INNER], F32, kind="ExternalInput")
    wg = nc.dram_tensor("wg", [D_INNER, D_INNER], F32, kind="ExternalInput")
    wxp = nc.dram_tensor("wxp", [D_INNER, 32], F32, kind="ExternalInput")
    biases = nc.dram_tensor("biases", [128, 49], F32, kind="ExternalInput")
    identw = nc.dram_tensor("identw", [128, 128], F32, kind="ExternalInput")

    o_u = nc.dram_tensor("o_u", [D_INNER, T], BF16, kind="ExternalOutput")
    o_sigz = nc.dram_tensor("o_sigz", [D_INNER, T], BF16, kind="ExternalOutput")
    o_bcdt = nc.dram_tensor("o_bcdt", [33, T], F32, kind="ExternalOutput")

    with ExitStack() as ctx:
        e = ctx.enter_context
        xnf = e(nc.sbuf_tensor("xnf", [128, KC * (T + 2)], BF16))
        ident = e(nc.sbuf_tensor("ident", [128, 128], BF16))
        onesb = e(nc.sbuf_tensor("onesb", [128, 1], BF16))
        wxpb = e(nc.sbuf_tensor("wxpb", [128, 16 * 32], BF16))
        biasv = e(nc.sbuf_tensor("biasv", [128, 49], F32))
        bcv = e(nc.sbuf_tensor("bcv", [33, T], F32))
        scr = {k: e(nc.sbuf_tensor("scr_" + k, [128, 17], F32))
               for k in ("mean", "var", "t1", "t2", "t3")}
        scr["sq"] = e(nc.sbuf_tensor("scr_sq", [128, D_MODEL], F32))
        ps = [e(nc.psum_tensor(f"ps{i}", [128, TS], F32)) for i in range(4)]
        psz = [e(nc.psum_tensor(f"psz{i}", [128, TS], F32)) for i in range(2)]
        pst = [e(nc.psum_tensor(f"pst{i}", [128, 1024], BF16)) for i in range(2)]
        sem_ld = e(nc.semaphore("s_ld"))
        sem_w0 = e(nc.semaphore("s_w0"))
        sem_w1 = e(nc.semaphore("s_w1"))
        sem_misc = e(nc.semaphore("s_misc"))
        sem_v = e(nc.semaphore("s_v"))
        sem_s = e(nc.semaphore("s_s"))
        sem_t = e(nc.semaphore("s_t"))
        sem_st = e(nc.semaphore("s_st"))
        sem_so = e(nc.semaphore("s_so"))
        LD = Ctr(sem_ld, 16)
        WL = [Ctr(sem_w0, 16), Ctr(sem_w1, 16)]
        V = Ctr(sem_v)
        SS = Ctr(sem_s)
        TT = Ctr(sem_t)
        ST = Ctr(sem_st, 16)
        SO = Ctr(sem_so, 16)
        epv, eps, ept, epg = EP(), EP(), EP(), EP()

        with nc.sbuf_tensor("xt", [128, 17 * D_MODEL], F32) as xt, \
             nc.sbuf_tensor("xnt", [128, 17 * D_MODEL], BF16) as xnt:
            bstk = ExitStack()
            blk = bstk.enter_context(nc.Block())

            @blk.sync
            def _(sync):
                for i in range(16):
                    LD.inc(sync.dma_start(xt[:, i * D_MODEL:(i + 1) * D_MODEL],
                                          xh[2 + i * 128: 2 + (i + 1) * 128, :]))
                LD.inc(sync.dma_start(xt[0:2, 16 * D_MODEL:17 * D_MODEL],
                                      xh[0:2, :]))
                LD.inc(sync.dma_start(biasv[:], biases.ap()))
            ld_all = LD.n

            @blk.gpsimd
            def _(g):
                g.dma_start(
                    wxpb[:].rearrange("p (kc s) -> p kc s", kc=16),
                    wxp.ap().rearrange("(kc p) s -> p kc s", p=128)).then_inc(sem_misc, 16)
                g.wait_ge(sem_misc, 16)
                g.dma_start(ident[:], identw.ap()).then_inc(sem_misc, 16)

            @blk.vector
            def _(v):
                v = epv.bind(v)
                v.wait_ge(sem_ld, ld_all)
                v.memset(onesb[:], 1.0)
                for i in range(17):
                    P = 128 if i < 16 else 2
                    xti = xt[:, i * D_MODEL:(i + 1) * D_MODEL]
                    c = lambda nm: scr[nm][:P, i:i + 1]
                    v.tensor_reduce(c("mean"), xti[:P, :], AXX, ALU.add)
                    v.tensor_scalar(c("mean"), c("mean"), 1.0 / 1024.0, None, ALU.mult)
                    v.tensor_tensor(scr["sq"][:P, :], xti[:P, :], xti[:P, :], ALU.mult)
                    v.tensor_reduce(c("var"), scr["sq"][:P, :], AXX, ALU.add)
                    v.tensor_scalar(c("var"), c("var"), 1.0 / 1024.0, None, ALU.mult)
                    v.tensor_tensor(c("t1"), c("mean"), c("mean"), ALU.mult)
                    v.tensor_sub(c("var"), c("var"), c("t1"))
                    v.tensor_scalar(c("var"), c("var"), LN_EPS, None, ALU.add)
                    v.reciprocal(c("t1"), c("var"))
                V.mark(v)
                v.wait_ge(sem_s, 1)
                for i in range(17):
                    P = 128 if i < 16 else 2
                    xti = xt[:, i * D_MODEL:(i + 1) * D_MODEL]
                    c = lambda nm: scr[nm][:P, i:i + 1]
                    v.tensor_tensor(c("t3"), c("t2"), c("t2"), ALU.mult)
                    v.tensor_tensor(c("t3"), c("t3"), c("var"), ALU.mult)
                    v.tensor_scalar(c("t3"), c("t3"), -0.5, 1.5, ALU.mult, ALU.add)
                    v.tensor_tensor(c("t2"), c("t2"), c("t3"), ALU.mult)
                    v.tensor_scalar(xnt[:P, i * D_MODEL:(i + 1) * D_MODEL],
                                    xti[:P, :], c("mean"), c("t2"),
                                    ALU.subtract, ALU.mult)
                V.mark(v)
            v_ln = V.n

            @blk.scalar
            def _(s):
                s = eps.bind(s)
                s.wait_ge(sem_v, 1)
                for i in range(17):
                    P = 128 if i < 16 else 2
                    s.activation(scr["t2"][:P, i:i + 1], scr["t1"][:P, i:i + 1],
                                 AF.Sqrt)
                SS.mark(s)

            @blk.tensor
            def _(t):
                t.wait_ge(sem_v, v_ln)
                t.wait_ge(sem_misc, 32)
                n = 0
                for i in range(17):
                    P = 128 if i < 16 else 2
                    for kc in range(KC):
                        if n >= 2:
                            t.wait_ge(sem_s, 1 + (n - 1))
                        t.transpose(pst[n % 2][:, 0:P],
                                    xnt[:P, i * D_MODEL + kc * 128:
                                        i * D_MODEL + (kc + 1) * 128],
                                    ident[:P, :P])
                        TT.mark(t)
                        n += 1

            @blk.scalar
            def _(s):
                s = eps.bind(s)
                n = 0
                for i in range(17):
                    P = 128 if i < 16 else 2
                    for kc in range(KC):
                        s.wait_ge(sem_t, n + 1)
                        dst = (2 + i * 128) if i < 16 else 0
                        SS.inc(s.activation(
                            xnf[:, kc * (T + 2) + dst:kc * (T + 2) + dst + P],
                            pst[n % 2][:, 0:P], AF.Copy))
                        n += 1
            s_tr = SS.n
            t_tr = TT.n
            bstk.close()

        wb = e(nc.sbuf_tensor("wb", [128, 2 * 24 * 128], BF16))
        wgb = e(nc.sbuf_tensor("wgb", [128, 2 * 16 * 128], BF16))
        xcv = e(nc.sbuf_tensor("xcv", [128, OC * T], BF16))
        uf = e(nc.sbuf_tensor("uf", [128, OC * T], BF16))

        bstk2 = ExitStack()
        blk2 = bstk2.enter_context(nc.Block())

        def wcol(buf, slot):
            return wb[:, buf * 3072 + slot * 128: buf * 3072 + (slot + 1) * 128]

        @blk2.gpsimd
        def _(g):
            for oc in range(OC):
                buf = oc % 2
                if oc >= 2:
                    g.wait_ge(sem_t, t_tr + (oc - 1) * NT)
                WL[buf].inc(g.dma_start(
                    wb[:, buf * 3072:buf * 3072 + 3072]
                    .rearrange("p (a o) -> p a o", a=24),
                    wconv.ap().rearrange("(a p) o -> p a o", p=128)
                    [:, :, oc * 128:(oc + 1) * 128]))
        wl_conv = [WL[0].n, WL[1].n]

        @blk2.tensor
        def _(t):
            t.wait_ge(sem_s, s_tr)
            for oc in range(OC):
                buf = oc % 2
                t.wait_ge([sem_w0, sem_w1][buf], 16 * (oc // 2 + 1))
                for ts in range(NT):
                    g_i = oc * NT + ts
                    if g_i >= 4:
                        t.wait_ge(sem_s, s_tr + (g_i - 4) + 1)
                    first = True
                    for tap in range(3):
                        for kc in range(KC):
                            rhs = xnf[:, kc * (T + 2) + tap + ts * TS:
                                      kc * (T + 2) + tap + ts * TS + TS]
                            t.matmul(ps[g_i % 4][:], wcol(buf, tap * KC + kc),
                                     rhs, start=first,
                                     stop=(tap == 2 and kc == KC - 1))
                            first = False
                    TT.mark(t)
        t_conv = TT.n

        @blk2.scalar
        def _(s):
            s = eps.bind(s)
            for g_i in range(OC * NT):
                oc, ts = divmod(g_i, NT)
                s.wait_ge(sem_t, t_tr + g_i + 1)
                SS.inc(s.activation(
                    xcv[:, oc * T + ts * TS: oc * T + ts * TS + TS],
                    ps[g_i % 4][:], AF.Identity, bias=biasv[:, oc:oc + 1]))
        s_conv = SS.n

        bstk2.close()
        bstk3 = ExitStack()
        blk3 = bstk3.enter_context(nc.Block())

        @blk3.gpsimd
        def _(g):
            for oc in range(OC):
                buf = oc % 2
                if oc >= 2:
                    g.wait_ge(sem_t, t_conv + (oc - 1) * NT)
                WL[buf].inc(g.dma_start(
                    wb[:, buf * 3072:buf * 3072 + 1024]
                    .rearrange("p (a o) -> p a o", a=8),
                    wz.ap().rearrange("(a p) o -> p a o", p=128)
                    [:, :, oc * 128:(oc + 1) * 128]))
        wl_z = [WL[0].n, WL[1].n]

        @blk3.tensor
        def _(t):
            for oc in range(OC):
                buf = oc % 2
                t.wait_ge([sem_w0, sem_w1][buf],
                          wl_conv[buf] + 16 * (oc // 2 + 1))
                for ts in range(NT):
                    g_i = oc * NT + ts
                    if g_i >= 2:
                        t.wait_ge(sem_s, s_conv + (g_i - 2) + 1)
                    for kc in range(KC):
                        rhs = xnf[:, kc * (T + 2) + 2 + ts * TS:
                                  kc * (T + 2) + 2 + ts * TS + TS]
                        t.matmul(psz[g_i % 2][:], wcol(buf, kc), rhs,
                                 start=(kc == 0), stop=(kc == KC - 1))
                    TT.mark(t)
        t_z = TT.n

        @blk3.scalar
        def _(s):
            s = eps.bind(s)
            for g_i in range(OC * NT):
                oc, ts = divmod(g_i, NT)
                s.wait_ge(sem_t, t_conv + g_i + 1)
                SS.inc(s.activation(
                    uf[:, oc * T + ts * TS: oc * T + ts * TS + TS],
                    psz[g_i % 2][:], AF.Sigmoid,
                    bias=biasv[:, 32 + oc:32 + oc + 1]))
        s_z = SS.n

        @blk3.gpsimd
        def _(g):
            for oc in range(OC):
                g.wait_ge(sem_s, s_conv + (oc + 1) * NT)
                ST.inc(g.dma_start(o_sigz[oc * 128:(oc + 1) * 128, :],
                                   uf[:, oc * T:(oc + 1) * T]))
        st_sigz = ST.n

        bstk3.close()
        bstk4 = ExitStack()
        blk4 = bstk4.enter_context(nc.Block())

        def wgcol(buf, kc):
            return wgb[:, buf * 2048 + kc * 128: buf * 2048 + (kc + 1) * 128]

        @blk4.gpsimd
        def _(g):
            for oc in range(OC):
                buf = oc % 2
                if oc >= 2:
                    g.wait_ge(sem_t, t_z + (oc - 1) * NT)
                WL[buf].inc(g.dma_start(
                    wgb[:, buf * 2048:buf * 2048 + 2048]
                    .rearrange("p (a o) -> p a o", a=16),
                    wg.ap().rearrange("(a p) o -> p a o", p=128)
                    [:, :, oc * 128:(oc + 1) * 128]))
        wl_g = [WL[0].n, WL[1].n]

        @blk4.tensor
        def _(t):
            t.wait_ge(sem_st, st_sigz)
            for oc in range(OC):
                buf = oc % 2
                t.wait_ge([sem_w0, sem_w1][buf],
                          wl_z[buf] + 16 * (oc // 2 + 1))
                for ts in range(NT):
                    g_i = oc * NT + ts
                    if g_i >= 2:
                        t.wait_ge(sem_s, s_z + (g_i - 2) + 1)
                    for kc in range(16):
                        t.matmul(psz[g_i % 2][:], wgcol(buf, kc),
                                 xcv[:, kc * T + ts * TS: kc * T + ts * TS + TS],
                                 start=(kc == 0), stop=(kc == 15))
                    TT.mark(t)
            for ts in range(NT):
                if ts >= 1:
                    t.wait_ge(sem_s, s_z + OC * NT + 2 * ts)
                for kc in range(16):
                    t.matmul(ps[0][0:32, :], wxpb[:, kc * 32:(kc + 1) * 32],
                             xcv[:, kc * T + ts * TS: kc * T + ts * TS + TS],
                             start=(kc == 0), stop=(kc == 15))
                TT.mark(t)
                for kc in range(16):
                    t.matmul(ps[1][0:1, :], onesb[:],
                             xcv[:, kc * T + ts * TS: kc * T + ts * TS + TS],
                             start=(kc == 0), stop=(kc == 15))
                TT.mark(t)
        t_gate = t_z + OC * NT
        t_xp = TT.n

        @blk4.scalar
        def _(s):
            s = eps.bind(s)
            for g_i in range(OC * NT):
                oc, ts = divmod(g_i, NT)
                s.wait_ge(sem_t, t_z + g_i + 1)
                SS.inc(s.activation(
                    uf[:, oc * T + ts * TS: oc * T + ts * TS + TS],
                    psz[g_i % 2][:], AF.Sigmoid,
                    bias=biasv[:, 16 + oc:16 + oc + 1]))
            for ts in range(NT):
                s.wait_ge(sem_t, t_gate + 2 * ts + 1)
                SS.inc(s.activation(bcv[0:32, ts * TS:(ts + 1) * TS],
                                    ps[0][0:32, :], AF.Identity,
                                    bias=biasv[0:32, 48:49]))
                s.wait_ge(sem_t, t_gate + 2 * ts + 2)
                SS.inc(s.activation(bcv[32:33, ts * TS:(ts + 1) * TS],
                                    ps[1][0:1, :], AF.Sigmoid,
                                    scale=1.0 / 2048.0))
        s_gate = s_z + OC * NT
        s_all = SS.n

        @blk4.vector
        def _(v):
            v = epv.bind(v)
            for oc in range(OC):
                v.wait_ge(sem_s, s_z + (oc + 1) * NT)
                v.tensor_tensor(uf[:, oc * T:(oc + 1) * T],
                                xcv[:, oc * T:(oc + 1) * T],
                                uf[:, oc * T:(oc + 1) * T], ALU.mult)
                V.mark(v)
            v.wait_ge(sem_s, s_all)
            v.tensor_scalar(bcv[32:33, :], bcv[32:33, :], 0.001, None, ALU.add)
            V.mark(v)
        v_all = V.n

        bstk4.close()
        bstk5 = ExitStack()
        blk5 = bstk5.enter_context(nc.Block())

        @blk5.sync
        def _(sync):
            for oc in range(OC):
                sync.wait_ge(sem_v, v_ln + oc + 1)
                SO.inc(sync.dma_start(o_u[oc * 128:(oc + 1) * 128, :],
                                      uf[:, oc * T:(oc + 1) * T]))
            sync.wait_ge(sem_v, v_all)
            SO.inc(sync.dma_start(o_bcdt.ap(), bcv[:]))

        @blk5.gpsimd
        def _(g):
            g.wait_ge(sem_st, ST.n)
            g.wait_ge(sem_so, SO.n)
        bstk5.close()
    return nc




def build_stage2():
    nc = bass.Bass("TRN2", target_bir_lowering=False)
    L = 127
    NCH = (S + L - 1) // L
    NF = 16 * 1024
    NSS = 16
    NEV = 5

    SPAD = S + 128
    u_d = nc.dram_tensor("u_d", [1024, SPAD], BF16, kind="ExternalInput")
    bcdt = nc.dram_tensor("bcdt", [33, S], F32, kind="ExternalInput")
    a_half = nc.dram_tensor("a_half", [1024, 16], F32, kind="ExternalInput")
    d_half = nc.dram_tensor("d_half", [1, 1024], F32, kind="ExternalInput")
    ltri_c = nc.dram_tensor("ltri_c", [128, 128], BF16, kind="ExternalInput")
    identw = nc.dram_tensor("identw", [128, 128], F32, kind="ExternalInput")
    zrowd = nc.dram_tensor("zrowd", [1, NF], BF16, kind="ExternalInput")
    onesd = nc.dram_tensor("onesd", [1, 2048], BF16, kind="ExternalInput")
    o_y = nc.dram_tensor("o_y", [S, 1024], BF16, kind="ExternalOutput")
    asp_d = nc.dram_tensor("asp_d", [16, 1024], F32, kind="Internal")

    def tok(k):
        t0 = k * L
        return t0, min(L, S - t0)

    with ExitStack() as ctx:
        e = ctx.enter_context
        arep = e(nc.sbuf_tensor("arep", [128, NF], F32))
        utok = e(nc.sbuf_tensor("utok", [128, 2 * 1024], BF16))
        ycb = e(nc.sbuf_tensor("ycb", [128, 2 * 1024], BF16))
        durep = e(nc.sbuf_tensor("durep", [128, 1024], BF16))
        ltri = e(nc.sbuf_tensor("ltri", [128, 128], BF16))
        ltris = e(nc.sbuf_tensor("ltris", [128, 2 * 16 * 128], BF16))
        identf = e(nc.sbuf_tensor("identf", [128, 128], F32))
        onesr = e(nc.sbuf_tensor("onesr", [1, 128], F32))
        rows36 = e(nc.sbuf_tensor("rows36", [36, S], F32))
        cols = e(nc.sbuf_tensor("cols", [128, 128], F32))
        bmdt = e(nc.sbuf_tensor("bmdt", [128, 2 * 16], F32))
        pst = e(nc.psum_tensor("pst", [128, 512], F32))
        sem_ld = e(nc.semaphore("s_ld"))
        sem_ld2 = e(nc.semaphore("s_ld2"))
        sem_ut = [e(nc.semaphore(f"s_ut{i}")) for i in range(3)]
        sem_sl = e(nc.semaphore("s_sl"))
        sem_g = e(nc.semaphore("s_g"))
        sem_v = e(nc.semaphore("s_v"))
        sem_s = e(nc.semaphore("s_s"))
        sem_t = e(nc.semaphore("s_t"))
        sem_so = [e(nc.semaphore(f"s_so{i}")) for i in range(2)]
        LD = Ctr(sem_ld, 16)
        LD2 = Ctr(sem_ld2, 16)
        GD = Ctr(sem_g, 16)
        V = Ctr(sem_v)
        SS = Ctr(sem_s)
        TT = Ctr(sem_t)
        epv, eps = EP(), EP()

        pstk = ExitStack()
        pe = pstk.enter_context
        asp = pe(nc.sbuf_tensor("asp", [128, 8 * 16], F32))
        dvrow = pe(nc.sbuf_tensor("dvrow", [1, 1024], F32))
        aspt = pe(nc.sbuf_tensor("aspt", [16, 1024], F32))
        dtr = pe(nc.sbuf_tensor("dtr", [1, S], F32))
        crow = pe(nc.sbuf_tensor("crow", [1, S], F32))
        ncp = pe(nc.sbuf_tensor("ncp", [1, S], F32))
        zrow = pe(nc.sbuf_tensor("zrow", [1, S], F32))
        psb = pe(nc.psum_tensor("psb", [128, 1024], F32))
        bstk = ExitStack()
        blk = bstk.enter_context(nc.Block())

        @blk.sync
        def _(sync):
            LD.inc(sync.dma_start(identf[:], identw.ap()))
            LD.inc(sync.dma_start(ltri[:], ltri_c.ap()))
            LD.inc(sync.dma_start(rows36[3:36, :], bcdt.ap()))
            LD.inc(sync.dma_start(dtr[:], bcdt[32:33, :]))
            LD.inc(sync.dma_start(dvrow[:], d_half.ap()))
            LD.inc(sync.dma_start(rows36[2:3, :], bcdt[32:33, :]))
            for i in range(8):
                LD.inc(sync.dma_start(asp[:, i * 16:(i + 1) * 16],
                                      a_half[i * 128:(i + 1) * 128, :]))
        ld_prep = LD.n

        @blk.vector
        def _(v):
            v = epv.bind(v)
            v.memset(onesr[:], 1.0)
            v.memset(zrow[:], 0.0)
            v.wait_ge(sem_s, 1)
            u_ = asp[:, 0:128]
            p_ = cols[:, 0:128]
            v.tensor_scalar(p_, u_, -1.0 / 6.0, 1.0 / 5.0, ALU.mult, ALU.add)
            for coef in (1.0 / 4.0, 1.0 / 3.0, 1.0 / 2.0, 1.0):
                v.tensor_tensor(p_, p_, u_, ALU.mult)
                v.tensor_scalar(p_, p_, -1.0, coef, ALU.mult, ALU.add)
            v.tensor_tensor(asp[:, 0:128], p_, u_, ALU.mult)
            V.mark(v)
            v.wait_ge(sem_ld, ld_prep)
            v.tensor_tensor_scan(crow[:], dtr[:], zrow[:], 0.0, ALU.add, ALU.add)
            v.memset(ncp[0:1, 0:1], 0.0)
            v.tensor_scalar(ncp[0:1, 1:S], crow[0:1, 0:S - 1], -1.0, None, ALU.mult)
            V.mark(v)

        @blk.scalar
        def _(s):
            s = eps.bind(s)
            s.wait_ge(sem_ld, ld_prep)
            for i in range(8):
                s.activation(asp[:, i * 16:(i + 1) * 16],
                             asp[:, i * 16:(i + 1) * 16], AF.Exp)
            SS.mark(s)

        @blk.tensor
        def _(t):
            t.wait_ge(sem_ld, ld_prep)
            t.wait_ge(sem_v, 1)
            for i in range(8):
                if i >= 1:
                    t.wait_ge(sem_s, 1 + i)
                t.transpose(pst[0:16, 0:128], asp[:, i * 16:(i + 1) * 16],
                            identf[:])
                t.drain().then_inc(sem_t, 1)
                TT.n += 1
            t.matmul(psb[:, 0:512], onesr[:], dvrow[0:1, 0:512],
                     start=True, stop=True)
            t.matmul(psb[:, 512:1024], onesr[:], dvrow[0:1, 512:1024],
                     start=True, stop=True)
            t.drain().then_inc(sem_t, 1)
            TT.n += 1
        tt_prep = TT.n

        @blk.scalar
        def _(s):
            s = eps.bind(s)
            for i in range(8):
                s.wait_ge(sem_t, i + 1)
                SS.inc(s.activation(aspt[:, i * 128:(i + 1) * 128],
                                    pst[0:16, 0:128], AF.Copy))
            s.wait_ge(sem_t, tt_prep)
            SS.inc(s.activation(durep[:], psb[:, 0:1024], AF.Copy))
        ss_prep = SS.n

        @blk.sync
        def _(sync):
            sync.wait_ge(sem_s, ss_prep)
            LD2.inc(sync.dma_start(asp_d.ap(), aspt[:]))
            sync.wait_ge(sem_v, 2)
            LD2.inc(sync.dma_start(rows36[0:1, :], ncp[:]))
            LD2.inc(sync.dma_start(rows36[1:2, :], crow[:]))
        ld_rows = LD2.n

        @blk.gpsimd
        def _(g):
            g.wait_ge(sem_ld2, ld_rows)
            g.dma_start(ltris[127:128, 0:2048], onesd.ap()).then_inc(sem_g, 16)
            g.dma_start(ltris[127:128, 2048:4096], onesd.ap()).then_inc(sem_g, 16)
            g.dma_start(cols[127:128, 19:35], onesd[0:1, 0:16]).then_inc(sem_g, 16)
            g.dma_start(cols[127:128, 59:75], onesd[0:1, 0:16]).then_inc(sem_g, 16)
            GD.n = 16 * 4
            for sidx in range(16):
                if sidx >= 1:
                    g.wait_ge(sem_g, GD.n)
                g.dma_start(arep[:, sidx * 1024:(sidx + 1) * 1024],
                            asp_d[sidx:sidx + 1, :].broadcast_to((128, 1024))
                            ).then_inc(sem_g, 16)
                GD.n += 16
        gd_arep = GD.n
        bstk.close()
        pstk.close()
        e0full = e(nc.sbuf_tensor("e0full", [128, NF], BF16))
        egb = e(nc.sbuf_tensor("egb", [128, NF], BF16))
        csb = e(nc.sbuf_tensor("csb", [128, NF], BF16))
        psw = [e(nc.psum_tensor(f"psw{i}", [128, 1024], F32)) for i in range(3)]

        rec_uses = []
        emitted = {}

        def use(key, val):
            rec_uses.append((key, val))
            return val

        def SBASE(k):
            return ss_prep + (0 if k == 0 else 2 + (k - 1) * 13)

        def s_einv_f(k):
            return SBASE(k) + (1 if k == 0 else 12)

        def s_e0_f(k):
            return SBASE(k) + (2 if k == 0 else 13)

        def s_ev_f(k, sx):
            assert sx >= NEV
            if k + 1 < NCH:
                return SBASE(k + 1) + (sx - NEV) + 1
            return ss_prep + 2 + (NCH - 1) * 13 + (sx - NEV) + 1

        def VBASE(k):
            return 2 + (0 if k == 0 else 34 + (k - 1) * 61)

        def v_cols_f(k):
            return VBASE(k) + 1

        def v_ltris_f(k, sx):
            return VBASE(k) + 2 + sx + 1

        def v_ev_f(k, sx):
            assert sx < NEV
            return VBASE(k + 1) + (18 if k + 1 < NCH else 0) + sx + 1

        def v_zdone_f(k):
            return VBASE(k + 1) + (39 if k + 1 < NCH else 21)

        def v_fin_f(k):
            return VBASE(k + 1) + (45 if k + 1 < NCH else 27)

        def v_g_f(k, sx):
            return VBASE(k) + (18 if k == 0 else 45) + sx + 1

        def TBASE(k):
            return tt_prep + 1 + k * 17

        def t_tr_f(k):
            return tt_prep + 1 if k == 0 else TBASE(k - 1) + 1

        def t_mm_f(k, sx):
            return TBASE(k) + (1 if k + 1 < NCH else 0) + sx + 1

        bstk1 = ExitStack()
        blk1 = bstk1.enter_context(nc.Block())

        @blk1.sync
        def _(sync):
            sync.dma_start(egb[127:128, :], zrowd.ap()).then_inc(sem_sl, 16)
            for k in range(NCH):
                t0, ln = tok(k)
                if k >= 2:
                    sync.wait_ge(sem_v, use(("vfin", k - 2), v_fin_f(k - 2)))
                sync.dma_start(
                    utok[:, (k % 2) * 1024:(k % 2) * 1024 + 1024],
                    u_d[:, t0:t0 + 128],
                    transpose=True).then_inc(sem_ut[k % 3], 16)
                if k >= 1:
                    sync.wait_ge(sem_s, use(("sev", k - 1, 15),
                                            s_ev_f(k - 1, 15)))
                    sync.wait_ge(sem_v, use(("vev", k - 1, NEV - 1),
                                            v_ev_f(k - 1, NEV - 1)))
                    sync.wait_ge(sem_t, use(("tmm", k - 1, 15),
                                            t_mm_f(k - 1, 15)))
                    sync.dma_start(egb[127:128, :],
                                   csb[127:128, :]).then_inc(sem_sl, 16)
                    pt0, pln = tok(k - 1)
                    sync.wait_ge(sem_v, use(("vfin", k - 1), v_fin_f(k - 1)))
                    sync.dma_start(
                        o_y[pt0:pt0 + pln, :],
                        ycb[0:pln, ((k - 1) % 2) * 1024:((k - 1) % 2) * 1024 + 1024]
                    ).then_inc(sem_so[(k - 1) % 2], 16)
            t0l, lnl = tok(NCH - 1)
            sync.wait_ge(sem_v, use(("vfin", NCH - 1), v_fin_f(NCH - 1)))
            sync.dma_start(
                o_y[t0l:t0l + lnl, :],
                ycb[0:lnl, ((NCH - 1) % 2) * 1024:((NCH - 1) % 2) * 1024 + 1024]
            ).then_inc(sem_so[(NCH - 1) % 2], 16)

        @blk1.scalar
        def _(s):
            s = eps.bind(s)
            s.wait_ge(sem_g, gd_arep)
            for k in range(NCH):
                t0, ln = tok(k)
                cb = (k % 2) * 40
                if k >= 1:
                    pt0, pln = tok(k - 1)
                    cbp = ((k - 1) % 2) * 40
                    for sx in range(NEV, NSS):
                        s.wait_ge(sem_t, use(("tmm", k - 1, sx),
                                             t_mm_f(k - 1, sx)))
                        SS.inc(s.activation(
                            csb[:, sx * 1024:(sx + 1) * 1024],
                            psw[sx % 3][:, :], AF.Identity,
                            scale=cols[:, cbp + 19 + sx:cbp + 20 + sx]))
                        emitted[("sev", k - 1, sx)] = SS.n
                s.wait_ge(sem_v, use(("vcols", k), v_cols_f(k)))
                SS.inc(s.activation(egb[0:ln, :], arep[0:ln, :], AF.Exp,
                                    scale=cols[0:ln, cb:cb + 1]))
                emitted[("seinv", k)] = SS.n
                if k >= 1:
                    s.wait_ge(sem_v, use(("vzd", k - 1), v_zdone_f(k - 1)))
                SS.inc(s.activation(e0full[0:ln, :], arep[0:ln, :], AF.Exp,
                                    scale=cols[0:ln, cb + 1:cb + 2]))
                emitted[("se0", k)] = SS.n
            cbl = ((NCH - 1) % 2) * 40
            for sx in range(NEV, NSS):
                s.wait_ge(sem_t, use(("tmm", NCH - 1, sx),
                                     t_mm_f(NCH - 1, sx)))
                SS.inc(s.activation(csb[:, sx * 1024:(sx + 1) * 1024],
                                    psw[sx % 3][:, :], AF.Identity,
                                    scale=cols[:, cbl + 19 + sx:cbl + 20 + sx]))
                emitted[("sev", NCH - 1, sx)] = SS.n

        @blk1.vector
        def _(v):
            v = epv.bind(v)
            v.wait_ge(sem_g, gd_arep)
            for k in range(NCH + 1):
                if k < NCH:
                    t0, ln = tok(k)
                    cb = (k % 2) * 40
                    lb = (k % 2) * 2048
                    v.wait_ge(sem_t, use(("ttr", k), t_tr_f(k)))
                    V.inc(v.tensor_scalar(cols[0:ln, cb:cb + 36],
                                          pst[0:ln, 0:36], 1.0, None,
                                          ALU.mult))
                    emitted[("vcols", k)] = V.n
                    V.inc(v.tensor_scalar(bmdt[0:ln, (k % 2) * 16:(k % 2) * 16 + 16],
                                          cols[0:ln, cb + 3:cb + 19],
                                          cols[0:ln, cb + 35:cb + 36],
                                          None, ALU.mult))
                    for sx in range(16):
                        V.inc(v.tensor_scalar(
                            ltris[0:L, lb + sx * 128:lb + (sx + 1) * 128],
                            ltri[0:L, :],
                            bmdt[0:L, (k % 2) * 16 + sx:(k % 2) * 16 + sx + 1],
                            None, ALU.mult))
                        emitted[("vltris", k, sx)] = V.n
                if k >= 1:
                    kp = k - 1
                    pt0, pln = tok(kp)
                    parp = kp % 2
                    for sx in range(NEV):
                        v.wait_ge(sem_t, use(("tmm", kp, sx), t_mm_f(kp, sx)))
                        V.inc(v.tensor_scalar(
                            csb[:, sx * 1024:(sx + 1) * 1024],
                            psw[sx % 3][:, :],
                            cols[:, (parp * 40) + 19 + sx:(parp * 40) + 20 + sx],
                            None, ALU.mult))
                        emitted[("vev", kp, sx)] = V.n
                    v.wait_ge(sem_s, use(("se0", kp), s_e0_f(kp)))
                    for sx in range(16):
                        if sx >= NEV:
                            v.wait_ge(sem_s, use(("sev", kp, sx),
                                                 s_ev_f(kp, sx)))
                        V.inc(v.tensor_tensor(
                            csb[0:pln, sx * 1024:(sx + 1) * 1024],
                            csb[0:pln, sx * 1024:(sx + 1) * 1024],
                            e0full[0:pln, sx * 1024:(sx + 1) * 1024],
                            ALU.mult))
                    emitted[("vzd", kp)] = V.n
                    V.inc(v.tensor_tensor(csb[0:pln, 0:8192],
                                          csb[0:pln, 0:8192],
                                          csb[0:pln, 8192:16384], ALU.add))
                    V.inc(v.tensor_tensor(csb[0:pln, 0:4096],
                                          csb[0:pln, 0:4096],
                                          csb[0:pln, 4096:8192], ALU.add))
                    V.inc(v.tensor_tensor(csb[0:pln, 0:2048],
                                          csb[0:pln, 0:2048],
                                          csb[0:pln, 2048:4096], ALU.add))
                    V.inc(v.tensor_tensor(csb[0:pln, 0:1024],
                                          csb[0:pln, 0:1024],
                                          csb[0:pln, 1024:2048], ALU.add))
                    du = ycb[0:pln, parp * 1024:parp * 1024 + 1024]
                    if k >= 3:
                        v.wait_ge(sem_so[(k - 3) % 2], 16 * ((k - 3) // 2 + 1))
                    V.inc(v.tensor_tensor(
                        du, utok[0:pln, parp * 1024:parp * 1024 + 1024],
                        durep[0:pln, :], ALU.mult))
                    V.inc(v.tensor_tensor(du, du, csb[0:pln, 0:1024], ALU.add))
                    emitted[("vfin", kp)] = V.n
                if k < NCH:
                    par = k % 2
                    v.wait_ge(sem_ut[k % 3], 16 * (k // 3 + 1))
                    v.wait_ge(sem_s, use(("seinv", k), s_einv_f(k)))
                    for sx in range(16):
                        V.inc(v.tensor_tensor(
                            egb[0:ln, sx * 1024:(sx + 1) * 1024],
                            utok[0:ln, par * 1024:par * 1024 + 1024],
                            egb[0:ln, sx * 1024:(sx + 1) * 1024],
                            ALU.mult))
                        emitted[("vg", k, sx)] = V.n

        @blk1.tensor
        def _(t):
            t.wait_ge(sem_s, ss_prep)
            t.wait_ge(sem_ld2, ld_rows)
            t.wait_ge(sem_g, gd_arep)

            def tr(kk):
                t0k, lnk = tok(kk)
                if kk >= 1:
                    t.wait_ge(sem_v, use(("vcols", kk - 1),
                                         v_cols_f(kk - 1)))
                t.transpose(pst[0:lnk, 0:36], rows36[:, t0k:t0k + lnk],
                            identf[0:36, 0:36])
                t.drain().then_inc(sem_t, 1)
                TT.n += 1
                emitted[("ttr", kk)] = TT.n

            tr(0)
            for k in range(NCH):
                lb = (k % 2) * 2048
                if k + 1 < NCH:
                    tr(k + 1)
                for sx in range(NSS):
                    t.wait_ge(sem_v, use(("vltris", k, sx),
                                         v_ltris_f(k, sx)))
                    t.wait_ge(sem_v, use(("vg", k, sx), v_g_f(k, sx)))
                    if sx == 0:
                        t.wait_ge(sem_sl, 16 * (k + 1))
                    gidx = k * NSS + sx - 3
                    if gidx >= 0:
                        pk, ps = divmod(gidx, NSS)
                        if ps < NEV:
                            t.wait_ge(sem_v, use(("vev", pk, ps),
                                                 v_ev_f(pk, ps)))
                        else:
                            t.wait_ge(sem_s, use(("sev", pk, ps),
                                                 s_ev_f(pk, ps)))
                    t.matmul(psw[sx % 3][:, 0:512],
                             ltris[:, lb + sx * 128:lb + (sx + 1) * 128],
                             egb[:, sx * 1024:sx * 1024 + 512],
                             start=True, stop=True)
                    t.matmul(psw[sx % 3][:, 512:1024],
                             ltris[:, lb + sx * 128:lb + (sx + 1) * 128],
                             egb[:, sx * 1024 + 512:sx * 1024 + 1024],
                             start=True, stop=True)
                    t.drain().then_inc(sem_t, 1)
                    TT.n += 1
                    emitted[("tmm", k, sx)] = TT.n

        @blk1.gpsimd
        def _(g):
            g.wait_ge(sem_so[0], 16 * ((NCH + 1) // 2))
            g.wait_ge(sem_so[1], 16 * (NCH // 2))
        bstk1.close()
        for key, val in rec_uses:
            assert emitted[key] == val, (key, val, emitted[key])
    return nc


def build_stage3():
    nc = bass.Bass("TRN2", target_bir_lowering=False)
    KC2 = 16
    y_own = nc.dram_tensor("y_own", [T, D_INNER], BF16, kind="ExternalInput")
    sigz = nc.dram_tensor("sigz", [D_INNER, T], BF16, kind="ExternalInput")
    x_own = nc.dram_tensor("x_own", [T, D_MODEL], F32, kind="ExternalInput")
    wo_t = nc.dram_tensor("wo_t", [128, KC2 * D_MODEL], F32,
                          kind="ExternalInput")
    bo_r = nc.dram_tensor("bo_r", [1, D_MODEL], F32, kind="ExternalInput")
    o_x2 = nc.dram_tensor("o_x2", [T, D_MODEL], F32, kind="ExternalOutput")
    o_xn2 = nc.dram_tensor("o_xn2", [T, D_MODEL], F32, kind="ExternalOutput")
    o_amax = nc.dram_tensor("o_amax", [1, 1], F32, kind="ExternalOutput")

    with ExitStack() as ctx:
        e = ctx.enter_context
        sgz = e(nc.sbuf_tensor("sgz", [128, 3 * T], BF16))
        ofe = e(nc.sbuf_tensor("ofe", [128, KC2 * T], BF16))
        wob = e(nc.sbuf_tensor("wob", [128, KC2 * D_MODEL], BF16))
        bor = e(nc.sbuf_tensor("bor", [1, D_MODEL], F32))
        onesr = e(nc.sbuf_tensor("onesr", [1, 128], F32))
        xt = e(nc.sbuf_tensor("xt", [128, 3 * D_MODEL], F32))
        x2t = e(nc.sbuf_tensor("x2t", [128, 2 * D_MODEL], F32))
        xn2t = e(nc.sbuf_tensor("xn2t", [128, 2 * D_MODEL], F32))
        amx = e(nc.sbuf_tensor("amx", [128, 17], F32))
        amr = e(nc.sbuf_tensor("amr", [1, 128], F32))
        amo = e(nc.sbuf_tensor("amo", [1, 1], F32))
        identf = e(nc.sbuf_tensor("identf", [128, 128], F32))
        scr = {k: e(nc.sbuf_tensor("scr_" + k, [128, 16], F32))
               for k in ("mean", "var", "t1", "t2", "t3")}
        scr["sq"] = e(nc.sbuf_tensor("scr_sq", [128, D_MODEL], F32))
        psw = [e(nc.psum_tensor(f"psw{i}", [128, D_MODEL], F32))
               for i in range(3)]
        pst = e(nc.psum_tensor("pst", [128, 512], F32))
        sem_ld = e(nc.semaphore("s_ld"))
        sem_sg = [e(nc.semaphore(f"s_sg{i}")) for i in range(3)]
        sem_y = [e(nc.semaphore(f"s_y{i}")) for i in range(4)]
        sem_wo = [e(nc.semaphore(f"s_wo{i}")) for i in range(8)]
        sem_v = e(nc.semaphore("s_v"))
        sem_s = e(nc.semaphore("s_s"))
        sem_t = e(nc.semaphore("s_t"))
        sem_md = e(nc.semaphore("s_md"))
        sem_x = [e(nc.semaphore(f"s_x{i}")) for i in range(3)]
        sem_sox = [e(nc.semaphore(f"s_sox{i}")) for i in range(2)]
        sem_son = [e(nc.semaphore(f"s_son{i}")) for i in range(2)]
        identw = nc.dram_tensor("identw", [128, 128], F32, kind="ExternalInput")
        LD = Ctr(sem_ld, 16)
        V = Ctr(sem_v)
        SS = Ctr(sem_s)
        TT = Ctr(sem_t)
        epv, eps = EP(), EP()
        vA, vB, vAdd = {}, {}, {}
        tTile = {i: i + 1 for i in range(16)}

        bstk = ExitStack()
        blk = bstk.enter_context(nc.Block())

        def _sync_body(sync):
            LD.inc(sync.dma_start(bor[:], bo_r.ap()))
            LD.inc(sync.dma_start(identf[:], identw.ap()))
            for kc in range(KC2):
                if kc >= 4:
                    sync.wait_ge(sem_v, kc - 2)
                sync.dma_start(
                    ofe[:, kc * T:(kc + 1) * T],
                    y_own[:, kc * 128:(kc + 1) * 128],
                    transpose=True).then_inc(sem_y[kc % 4], 16)
            for i in range(16):
                if i >= 3:
                    sync.wait_ge(sem_v, vA[i - 3])
                sync.dma_start(
                    xt[:, (i % 3) * D_MODEL:(i % 3) * D_MODEL + D_MODEL],
                    x_own[i * 128:(i + 1) * 128, :]).then_inc(sem_x[i % 3], 16)
                if i >= 1:
                    j = i - 1
                    sync.wait_ge(sem_v, vA[j])
                    sync.dma_start(
                        o_x2[j * 128:(j + 1) * 128, :],
                        x2t[:, (j % 2) * D_MODEL:(j % 2 + 1) * D_MODEL]
                    ).then_inc(sem_sox[j % 2], 16)
                    sync.wait_ge(sem_v, vB[j])
                    sync.dma_start(
                        o_xn2[j * 128:(j + 1) * 128, :],
                        xn2t[:, (j % 2) * D_MODEL:(j % 2 + 1) * D_MODEL]
                    ).then_inc(sem_son[j % 2], 16)
            sync.wait_ge(sem_v, vA[15])
            sync.dma_start(o_x2[15 * 128:T, :],
                           x2t[:, (15 % 2) * D_MODEL:(15 % 2 + 1) * D_MODEL]
                           ).then_inc(sem_sox[15 % 2], 16)
            sync.wait_ge(sem_v, vB[15])
            sync.dma_start(o_xn2[15 * 128:T, :],
                           xn2t[:, (15 % 2) * D_MODEL:(15 % 2 + 1) * D_MODEL]
                           ).then_inc(sem_son[15 % 2], 16)

        @blk.gpsimd
        def _(g):
            for kc in range(KC2):
                if kc >= 3:
                    g.wait_ge(sem_v, kc - 1)
                g.dma_start(sgz[:, (kc % 3) * T:(kc % 3 + 1) * T],
                            sigz[kc * 128:(kc + 1) * 128, :]
                            ).then_inc(sem_sg[kc % 3], 16)
                if 2 <= kc < 10:
                    j = kc - 2
                    g.dma_start(wob[:, j * 2048:(j + 1) * 2048],
                                wo_t[:, j * 2048:(j + 1) * 2048]
                                ).then_inc(sem_wo[j], 16)

        @blk.vector
        def _(v):
            v = epv.bind(v)
            v.memset(onesr[:], 1.0)
            V.mark(v)
            v.wait_ge(sem_ld, 2 * 16)
            for kc in range(KC2):
                v.wait_ge(sem_y[kc % 4], 16 * (kc // 4 + 1))
                v.wait_ge(sem_sg[kc % 3], 16 * (kc // 3 + 1))
                V.inc(v.tensor_tensor(ofe[:, kc * T:(kc + 1) * T],
                                      ofe[:, kc * T:(kc + 1) * T],
                                      sgz[:, (kc % 3) * T:(kc % 3 + 1) * T],
                                      ALU.mult))
            v_o = V.n
            for i in range(17):
                if i < 16:
                    par = (i % 2) * D_MODEL
                    pw = psw[i % 3]
                    v.wait_ge(sem_t, tTile[i])
                    v.wait_ge(sem_x[i % 3], 16 * (i // 3 + 1))
                    if i >= 2:
                        v.wait_ge(sem_sox[(i - 2) % 2], 16 * ((i - 2) // 2 + 1))
                    xti = x2t[:, par:par + D_MODEL]
                    par3 = (i % 3) * D_MODEL
                    v.tensor_tensor(x2t[:, par:par + 512],
                                    xt[:, par3:par3 + 512],
                                    pw[:, 0:512], ALU.add)
                    v.tensor_tensor(x2t[:, par + 512:par + D_MODEL],
                                    xt[:, par3 + 512:par3 + D_MODEL],
                                    pw[:, 512:1024], ALU.add)
                    V.mark(v)
                    vAdd[i] = V.n
                    c = lambda nm: scr[nm][:, i:i + 1]
                    v.tensor_reduce(c("mean"), xti, AXX, ALU.add)
                    v.tensor_scalar(c("mean"), c("mean"), 1.0 / 1024.0, None,
                                    ALU.mult)
                    v.tensor_tensor(scr["sq"][:, :], xti, xti, ALU.mult)
                    v.tensor_reduce(c("var"), scr["sq"][:, :], AXX, ALU.add)
                    v.tensor_scalar(c("var"), c("var"), 1.0 / 1024.0, None,
                                    ALU.mult)
                    v.tensor_tensor(c("t1"), c("mean"), c("mean"), ALU.mult)
                    v.tensor_sub(c("var"), c("var"), c("t1"))
                    v.tensor_scalar(c("var"), c("var"), LN_EPS, None, ALU.add)
                    v.reciprocal(c("t1"), c("var"))
                    V.mark(v)
                    vA[i] = V.n
                if i >= 1:
                    j = i - 1
                    parj = (j % 2) * D_MODEL
                    v.wait_ge(sem_s, j + 1)
                    if j >= 2:
                        v.wait_ge(sem_son[(j - 2) % 2],
                                  16 * ((j - 2) // 2 + 1))
                    xtj = x2t[:, parj:parj + D_MODEL]
                    c = lambda nm: scr[nm][:, j:j + 1]
                    v.tensor_tensor(c("t3"), c("t2"), c("t2"), ALU.mult)
                    v.tensor_tensor(c("t3"), c("t3"), c("var"), ALU.mult)
                    v.tensor_scalar(c("t3"), c("t3"), -0.5, 1.5, ALU.mult,
                                    ALU.add)
                    v.tensor_tensor(c("t2"), c("t2"), c("t3"), ALU.mult)
                    v.tensor_scalar(xn2t[:, parj:parj + D_MODEL], xtj,
                                    c("mean"), c("t2"), ALU.subtract, ALU.mult)
                    v.tensor_reduce(amx[:, j:j + 1],
                                    xn2t[:, parj:parj + D_MODEL],
                                    AXX, ALU.max, apply_absolute_value=True)
                    V.mark(v)
                    vB[j] = V.n
            v.tensor_reduce(amx[:, 16:17], amx[:, 0:16], AXX, ALU.max)
            V.mark(v)
        v_fin = V.n

        @blk.tensor
        def _(t):
            t.wait_ge(sem_ld, 2 * 16)
            t.wait_ge(sem_v, 1)
            for i in range(16):
                pw = psw[i % 3]
                if i >= 3:
                    t.wait_ge(sem_v, vAdd[i - 3])
                if i == 1:
                    t.wait_ge(sem_v, 17)
                for nt in range(2):
                    for kc in range(KC2):
                        if i == 0 and nt == 0:
                            t.wait_ge(sem_y[kc % 4], 16 * (kc // 4 + 1))
                            t.wait_ge(sem_v, kc + 2)
                            t.wait_ge(sem_wo[kc // 2], 16)
                        t.matmul(pw[:, nt * 512:nt * 512 + 512],
                                 ofe[:, kc * T + i * 128: kc * T + (i + 1) * 128],
                                 wob[:, kc * D_MODEL + nt * 512:
                                     kc * D_MODEL + nt * 512 + 512],
                                 start=(kc == 0), stop=False)
                    t.matmul(pw[:, nt * 512:nt * 512 + 512],
                             onesr[0:1, 0:128],
                             bor[0:1, nt * 512:nt * 512 + 512],
                             start=False, stop=True)
                t.drain().then_inc(sem_t, 1)
                TT.n += 1
                assert tTile[i] == TT.n
            t.wait_ge(sem_v, v_fin)
            t.transpose(pst[0:1, 0:128], amx[:, 16:17], identf[:])
            t.drain().then_inc(sem_t, 1)
            TT.n += 1
        t_fin = TT.n

        @blk.scalar
        def _(s):
            s = eps.bind(s)
            for i in range(16):
                s.wait_ge(sem_v, vA[i])
                s.activation(scr["t2"][:, i:i + 1], scr["t1"][:, i:i + 1],
                             AF.Sqrt)
                SS.mark(s)

        @blk.vector
        def _(v):
            v = epv.bind(v)
            v.wait_ge(sem_t, t_fin)
            v.tensor_copy(amr[:], pst[0:1, 0:128])
            v.tensor_reduce(amo[:], amr[:], AXX, ALU.max)
            V.mark(v)
        v_amo = V.n

        @blk.gpsimd
        def _(g):
            g.wait_ge(sem_v, v_amo)
            g.dma_start(o_amax.ap(), amo[:]).then_inc(sem_md, 16)
            g.wait_ge(sem_md, 16)
            for q in range(2):
                g.wait_ge(sem_sox[q], 16 * 8)
                g.wait_ge(sem_son[q], 16 * 8)
        blk.sync(_sync_body)
        bstk.close()
    return nc


def build_stage4():
    nc = bass.Bass("TRN2", target_bir_lowering=False)
    OC, NT, TS, KC = 32, 4, 512, 8
    xn2 = nc.dram_tensor("xn2", [T, D_MODEL], F32, kind="ExternalInput")
    amax8 = nc.dram_tensor("amax8", [1, 8], F32, kind="ExternalInput")
    w_inT = nc.dram_tensor("w_inT", [4096, D_MODEL], F32, kind="ExternalInput")
    bin_c = nc.dram_tensor("bin_c", [128, 32], F32, kind="ExternalInput")
    identw = nc.dram_tensor("identw", [128, 128], F32, kind="ExternalInput")
    identwb = nc.dram_tensor("identwb", [128, 128], BF16, kind="ExternalInput")
    o_hid = nc.dram_tensor("o_hid", [4096, T], BF16, kind="ExternalOutput")
    o_amax = nc.dram_tensor("o_amax", [1, 1], F32, kind="ExternalOutput")

    with ExitStack() as ctx:
        e = ctx.enter_context
        kq = e(nc.sbuf_tensor("kq", [128, KC * 4096], BF16))
        rxf = e(nc.sbuf_tensor("rxf", [128, KC * T], BF16))
        hidb = e(nc.sbuf_tensor("hidb", [128, 4 * T], BF16))
        wqt = e(nc.sbuf_tensor("wqt", [128, 2 * D_MODEL], F32))
        kqt = e(nc.sbuf_tensor("kqt", [128, 2 * D_MODEL], BF16))
        xtt = e(nc.sbuf_tensor("xtt", [128, 2 * D_MODEL], F32))
        rxt = e(nc.sbuf_tensor("rxt", [128, 2 * D_MODEL], BF16))
        colsc = e(nc.sbuf_tensor("colsc", [128, 32], F32))
        binv = e(nc.sbuf_tensor("binv", [128, 32], F32))
        am8 = e(nc.sbuf_tensor("am8", [1, 8], F32))
        sx1 = e(nc.sbuf_tensor("sx1", [1, 2], F32))
        sxc = e(nc.sbuf_tensor("sxc", [128, 2], F32))
        onesr = e(nc.sbuf_tensor("onesr", [1, 128], F32))
        identf = e(nc.sbuf_tensor("identf", [128, 128], F32))
        identb = e(nc.sbuf_tensor("identb", [128, 128], BF16))
        amx = e(nc.sbuf_tensor("amx", [128, 33], F32))
        amr = e(nc.sbuf_tensor("amr", [1, 128], F32))
        amo = e(nc.sbuf_tensor("amo", [1, 1], F32))
        wsc = e(nc.sbuf_tensor("wsc", [128, 3 * 32], F32))
        ps = [e(nc.psum_tensor(f"ps{i}", [128, TS], F32)) for i in range(4)]
        pst = [e(nc.psum_tensor(f"pst{i}", [128, 1024], BF16)) for i in range(2)]
        psb = e(nc.psum_tensor("psb", [128, 512], F32))
        sem_ld = e(nc.semaphore("s_ld"))
        sem_w0 = e(nc.semaphore("s_w0"))
        sem_w1 = e(nc.semaphore("s_w1"))
        sem_x0 = e(nc.semaphore("s_x0"))
        sem_x1 = e(nc.semaphore("s_x1"))
        sem_v = e(nc.semaphore("s_v"))
        sem_s = e(nc.semaphore("s_s"))
        sem_t = e(nc.semaphore("s_t"))
        sem_so = [e(nc.semaphore(f"s_so{i}")) for i in range(4)]
        sem_fin = e(nc.semaphore("s_fin"))
        LD = Ctr(sem_ld, 16)
        WL = [Ctr(sem_w0, 16), Ctr(sem_w1, 16)]
        XL = [Ctr(sem_x0, 16), Ctr(sem_x1, 16)]
        V = Ctr(sem_v)
        SS = Ctr(sem_s)
        TT = Ctr(sem_t)
        SO = Ctr(sem_so, 16)
        epv, eps = EP(), EP()

        bstk = ExitStack()
        blk = bstk.enter_context(nc.Block())

        @blk.sync
        def _(sync):
            LD.inc(sync.dma_start(am8[:], amax8.ap()))
            LD.inc(sync.dma_start(binv[:], bin_c.ap()))
            LD.inc(sync.dma_start(identf[:], identw.ap()))
            LD.inc(sync.dma_start(identb[:], identwb.ap()))
        ld_all = LD.n

        @blk.vector
        def _(v):
            v = epv.bind(v)
            v.memset(onesr[:], 1.0)
            v.wait_ge(sem_ld, ld_all)
            v.tensor_reduce(sx1[0:1, 1:2], am8[:], AXX, ALU.max)
            v.reciprocal(sx1[0:1, 0:1], sx1[0:1, 1:2])
            v.tensor_scalar(sx1[0:1, 0:1], sx1[0:1, 0:1], 32767.0, None, ALU.mult)
            v.tensor_scalar(sx1[0:1, 1:2], sx1[0:1, 1:2], 1.0 / 32767.0, None,
                            ALU.mult)
            V.mark(v)
        v_sx = V.n

        @blk.tensor
        def _(t):
            t.wait_ge(sem_v, v_sx)
            t.matmul(psb[:, 0:2], onesr[:], sx1[0:1, 0:2], start=True, stop=True)
            t.drain().then_inc(sem_t, 1)
            TT.n += 1
        t_sx = TT.n

        @blk.scalar
        def _(s):
            s = eps.bind(s)
            s.wait_ge(sem_t, t_sx)
            SS.inc(s.activation(sxc[:], psb[:, 0:2], AF.Copy))
        s_sx = SS.n

        @blk.gpsimd
        def _(g):
            for m in range(32):
                buf = m % 2
                if m >= 2:
                    g.wait_ge(sem_v, v_sx + (m - 1))
                WL[buf].inc(g.dma_start(
                    wqt[:, buf * D_MODEL:(buf + 1) * D_MODEL],
                    w_inT[m * 128:(m + 1) * 128, :]))

        @blk.vector
        def _(v):
            v = epv.bind(v)
            v.wait_ge(sem_s, s_sx)
            for m in range(32):
                buf = m % 2
                v.wait_ge([sem_w0, sem_w1][buf], 16 * (m // 2 + 1))
                if m >= 2:
                    v.wait_ge(sem_t, t_sx + (m - 1) * KC)
                wt = wqt[:, buf * D_MODEL:(buf + 1) * D_MODEL]
                v.tensor_reduce(amx[:, 32:33], wt, AXX, ALU.max,
                                apply_absolute_value=True)
                v.tensor_scalar_max(amx[:, 32:33], amx[:, 32:33], 1e-8)
                v.reciprocal(wsc[:, 64 + 0:64 + 1], amx[:, 32:33])
                v.tensor_scalar(wsc[:, 64:65], wsc[:, 64:65], 127.0, None,
                                ALU.mult)
                v.tensor_scalar(colsc[:, m:m + 1], amx[:, 32:33],
                                1.0 / 127.0, None, ALU.mult)
                v.tensor_tensor(colsc[:, m:m + 1], colsc[:, m:m + 1],
                                sxc[:, 1:2], ALU.mult)
                v.tensor_scalar(wt, wt, wsc[:, 64:65], MAGIC, ALU.mult, ALU.add)
                v.tensor_scalar(kqt[:, buf * D_MODEL:(buf + 1) * D_MODEL],
                                wt, -MAGIC, None, ALU.add)
                V.mark(v)
        v_wq = V.n

        @blk.tensor
        def _(t):
            for m in range(32):
                buf = m % 2
                t.wait_ge(sem_v, v_sx + m + 1)
                for kc in range(KC):
                    n = m * KC + kc
                    if n >= 2:
                        t.wait_ge(sem_s, s_sx + (n - 1))
                    t.transpose(pst[n % 2][:, 0:128],
                                kqt[:, buf * D_MODEL + kc * 128:
                                    buf * D_MODEL + (kc + 1) * 128],
                                identb[:])
                    t.drain().then_inc(sem_t, 1)
                    TT.n += 1
        t_tr = TT.n

        @blk.scalar
        def _(s):
            s = eps.bind(s)
            for m in range(32):
                for kc in range(KC):
                    n = m * KC + kc
                    s.wait_ge(sem_t, t_sx + n + 1)
                    SS.inc(s.activation(
                        kq[:, kc * 4096 + m * 128: kc * 4096 + (m + 1) * 128],
                        pst[n % 2][:, 0:128], AF.Copy))
        s_kq = SS.n
        bstk.close()

        bstk2 = ExitStack()
        blk2 = bstk2.enter_context(nc.Block())

        @blk2.sync
        def _(sync):
            for i in range(16):
                if i >= 2:
                    sync.wait_ge(sem_v, v_wq + (i - 1))
                XL[i % 2].inc(sync.dma_start(
                    xtt[:, (i % 2) * D_MODEL:(i % 2 + 1) * D_MODEL],
                    xn2[i * 128:(i + 1) * 128, :]))

        @blk2.vector
        def _(v):
            v = epv.bind(v)
            for i in range(16):
                buf = i % 2
                v.wait_ge([sem_x0, sem_x1][buf], 16 * (i // 2 + 1))
                if i >= 2:
                    v.wait_ge(sem_t, t_tr + (i - 1) * KC)
                xti = xtt[:, buf * D_MODEL:(buf + 1) * D_MODEL]
                v.tensor_scalar(xti, xti, sxc[:, 0:1], MAGIC, ALU.mult, ALU.add)
                v.tensor_scalar(rxt[:, buf * D_MODEL:(buf + 1) * D_MODEL],
                                xti, -MAGIC, None, ALU.add)
                V.mark(v)
        v_rx = V.n

        @blk2.tensor
        def _(t):
            for i in range(16):
                buf = i % 2
                t.wait_ge(sem_v, v_wq + i + 1)
                for kc in range(KC):
                    n = i * KC + kc
                    if n >= 2:
                        t.wait_ge(sem_s, s_kq + (n - 1))
                    t.transpose(pst[n % 2][:, 0:128],
                                rxt[:, buf * D_MODEL + kc * 128:
                                    buf * D_MODEL + (kc + 1) * 128],
                                identb[:])
                    t.drain().then_inc(sem_t, 1)
                    TT.n += 1
        t_rx = TT.n

        @blk2.scalar
        def _(s):
            s = eps.bind(s)
            for i in range(16):
                for kc in range(KC):
                    n = i * KC + kc
                    s.wait_ge(sem_t, t_tr + n + 1)
                    SS.inc(s.activation(
                        rxf[:, kc * T + i * 128: kc * T + (i + 1) * 128],
                        pst[n % 2][:, 0:128], AF.Copy))
        s_rx = SS.n
        bstk2.close()

        bstk3 = ExitStack()
        blk3 = bstk3.enter_context(nc.Block())

        @blk3.tensor
        def _(t):
            t.wait_ge(sem_s, s_rx)
            for oc in range(OC):
                for ts in range(NT):
                    g_i = oc * NT + ts
                    if g_i >= 4:
                        t.wait_ge(sem_s, s_rx + (g_i - 4) + 1)
                    for kc in range(KC):
                        t.matmul(ps[g_i % 4][:],
                                 kq[:, kc * 4096 + oc * 128:
                                     kc * 4096 + (oc + 1) * 128],
                                 rxf[:, kc * T + ts * TS: kc * T + ts * TS + TS],
                                 start=(kc == 0), stop=(kc == KC - 1))
                    t.drain().then_inc(sem_t, 1)
                    TT.n += 1
        t_mm = TT.n

        @blk3.scalar
        def _(s):
            s = eps.bind(s)
            for g_i in range(OC * NT):
                oc, ts = divmod(g_i, NT)
                s.wait_ge(sem_t, t_rx + g_i + 1)
                if oc >= 4 and ts == 0:
                    s.wait_ge(sem_so[oc % 4], 16 * (oc // 4))
                SS.inc(s.activation(
                    hidb[:, (oc % 4) * T + ts * TS:(oc % 4) * T + ts * TS + TS],
                    ps[g_i % 4][:], AF.Gelu,
                    bias=binv[:, oc:oc + 1], scale=colsc[:, oc:oc + 1]))
        s_gl = SS.n

        @blk3.vector
        def _(v):
            v = epv.bind(v)
            for oc in range(OC):
                v.wait_ge(sem_s, s_rx + oc * NT + NT)
                v.tensor_reduce(amx[:, oc:oc + 1],
                                hidb[:, (oc % 4) * T:(oc % 4) * T + T],
                                AXX, ALU.max, apply_absolute_value=True)
                V.mark(v)
            v.tensor_reduce(amx[:, 32:33], amx[:, 0:32], AXX, ALU.max)
            V.mark(v)
        v_am = V.n

        @blk3.sync
        def _(sync):
            for oc in range(OC):
                sync.wait_ge(sem_v, v_rx + oc + 1)
                sync.dma_start(o_hid[oc * 128:(oc + 1) * 128, :],
                               hidb[:, (oc % 4) * T:(oc % 4) * T + T]
                               ).then_inc(sem_so[oc % 4], 16)
                SO.n += 16

        @blk3.tensor
        def _(t):
            t.wait_ge(sem_v, v_am)
            t.transpose(psb[0:1, 0:128], amx[:, 32:33], identf[:])
            t.drain().then_inc(sem_t, 1)
            TT.n += 1
        t_am = TT.n

        @blk3.vector
        def _(v):
            v = epv.bind(v)
            v.wait_ge(sem_t, t_am)
            v.tensor_copy(amr[:], psb[0:1, 0:128])
            v.tensor_reduce(amo[:], amr[:], AXX, ALU.max)
            V.mark(v)
        v_fin = V.n

        @blk3.gpsimd
        def _(g):
            g.wait_ge(sem_v, v_fin)
            g.dma_start(o_amax.ap(), amo[:]).then_inc(sem_fin, 16)
            g.wait_ge(sem_fin, 16)
            for q in range(4):
                g.wait_ge(sem_so[q], 16 * (OC // 4))
        bstk3.close()
    return nc


def build_stage5():
    nc = bass.Bass("TRN2", target_bir_lowering=False)
    KC = 32
    TH = T // 2
    hid = nc.dram_tensor("hid", [4096, T], BF16, kind="ExternalInput")
    amax8 = nc.dram_tensor("amax8", [1, 8], F32, kind="ExternalInput")
    w_out = nc.dram_tensor("w_out", [4096, D_MODEL], F32, kind="ExternalInput")
    bo_row = nc.dram_tensor("bo_row", [1, D_MODEL], F32, kind="ExternalInput")
    x2 = nc.dram_tensor("x2", [T, D_MODEL], F32, kind="ExternalInput")
    identw = nc.dram_tensor("identw", [128, 128], F32, kind="ExternalInput")
    o_out = nc.dram_tensor("o_out", [T, D_MODEL], F32, kind="ExternalOutput")

    with ExitStack() as ctx:
        e = ctx.enter_context
        wqb = e(nc.sbuf_tensor("wqb", [128, KC * D_MODEL], BF16))
        rhf = e(nc.sbuf_tensor("rhf", [128, KC * TH], BF16))
        wsb = e(nc.sbuf_tensor("wsb", [128, 2 * D_MODEL], F32))
        tmpf = e(nc.sbuf_tensor("tmpf", [128, 2 * TH], F32))
        hbb = e(nc.sbuf_tensor("hbb", [128, 2 * TH], BF16))
        bor = e(nc.sbuf_tensor("bor", [1, D_MODEL], F32))
        brow = e(nc.sbuf_tensor("brow", [1, D_MODEL], F32))
        am8 = e(nc.sbuf_tensor("am8", [1, 8], F32))
        sx1 = e(nc.sbuf_tensor("sx1", [1, 2], F32))
        sxc = e(nc.sbuf_tensor("sxc", [128, 2], F32))
        onesr = e(nc.sbuf_tensor("onesr", [1, 128], F32))
        wamx = e(nc.sbuf_tensor("wamx", [128, 2], F32))
        xt = e(nc.sbuf_tensor("xt", [128, 3 * D_MODEL], F32))
        ot = e(nc.sbuf_tensor("ot", [128, 2 * D_MODEL], F32))
        psw = [e(nc.psum_tensor(f"psw{i}", [128, D_MODEL], F32))
               for i in range(3)]
        psb = e(nc.psum_tensor("psb", [128, 512], F32))
        sem_ld = e(nc.semaphore("s_ld"))
        sem_w0 = e(nc.semaphore("s_w0"))
        sem_w1 = e(nc.semaphore("s_w1"))
        sem_h0 = e(nc.semaphore("s_h0"))
        sem_h1 = e(nc.semaphore("s_h1"))
        sem_v = e(nc.semaphore("s_v"))
        sem_s = e(nc.semaphore("s_s"))
        sem_t = e(nc.semaphore("s_t"))
        sem_x = [e(nc.semaphore(f"s_x{i}")) for i in range(3)]
        sem_so = [e(nc.semaphore(f"s_so{i}")) for i in range(2)]
        LD = Ctr(sem_ld, 16)
        V = Ctr(sem_v)
        SS = Ctr(sem_s)
        TT = Ctr(sem_t)
        epv, eps = EP(), EP()

        def use5(key, val):
            return val

        bstk = ExitStack()
        blk = bstk.enter_context(nc.Block())

        @blk.sync
        def _(sync):
            LD.inc(sync.dma_start(am8[:], amax8.ap()))
            LD.inc(sync.dma_start(bor[:], bo_row.ap()))
        ld_all = LD.n

        @blk.vector
        def _(v):
            v = epv.bind(v)
            v.memset(onesr[:], 1.0)
            v.wait_ge(sem_ld, ld_all)
            v.tensor_reduce(sx1[0:1, 1:2], am8[:], AXX, ALU.max)
            v.reciprocal(sx1[0:1, 0:1], sx1[0:1, 1:2])
            v.tensor_scalar(sx1[0:1, 0:1], sx1[0:1, 0:1], 32767.0, None, ALU.mult)
            v.tensor_scalar(sx1[0:1, 1:2], sx1[0:1, 1:2], 1.0 / 32767.0, None,
                            ALU.mult)
            v.tensor_scalar(brow[:], bor[:], sx1[0:1, 0:1], None, ALU.mult)
            V.mark(v)
        v_sx = V.n

        @blk.tensor
        def _(t):
            t.wait_ge(sem_v, v_sx)
            t.matmul(psb[:, 0:2], onesr[:], sx1[0:1, 0:2], start=True, stop=True)
            t.drain().then_inc(sem_t, 1)
            TT.n += 1
        t_sx = TT.n

        @blk.scalar
        def _(s):
            s = eps.bind(s)
            s.wait_ge(sem_t, t_sx)
            SS.inc(s.activation(sxc[:], psb[:, 0:2], AF.Copy))
        s_sx = SS.n

        @blk.gpsimd
        def _(g):
            for kc in range(KC):
                buf = kc % 2
                if kc >= 2:
                    g.wait_ge(sem_v, v_sx + (kc - 1))
                g.dma_start(
                    wsb[:, buf * D_MODEL: buf * D_MODEL + D_MODEL],
                    w_out[kc * 128:(kc + 1) * 128, :]
                ).then_inc([sem_w0, sem_w1][buf], 16)

        @blk.vector
        def _(v):
            v = epv.bind(v)
            for kc in range(KC):
                buf = kc % 2
                v.wait_ge([sem_w0, sem_w1][buf], 16 * (kc // 2 + 1))
                wt = wsb[:, buf * D_MODEL: buf * D_MODEL + D_MODEL]
                v.tensor_reduce(wamx[:, 0:1], wt, AXX, ALU.max,
                                apply_absolute_value=True)
                v.tensor_scalar_max(wamx[:, 0:1], wamx[:, 0:1], 1e-8)
                v.reciprocal(wamx[:, 1:2], wamx[:, 0:1])
                v.tensor_scalar(wamx[:, 1:2], wamx[:, 1:2], 127.0, None, ALU.mult)
                v.tensor_scalar(wamx[:, 0:1], wamx[:, 0:1], 1.0 / 127.0, None,
                                ALU.mult)
                v.tensor_scalar(wt, wt, wamx[:, 1:2], MAGIC, ALU.mult, ALU.add)
                v.tensor_scalar(wt, wt, -MAGIC, None, ALU.add)
                v.tensor_scalar(wqb[:, kc * D_MODEL:(kc + 1) * D_MODEL],
                                wt, wamx[:, 0:1], None, ALU.mult)
                V.mark(v)
        v_wq = V.n

        v_rhm, v_out, t_tile = {}, {}, {}

        @blk.vector
        def _(v):
            v = epv.bind(v)
            v.wait_ge(sem_s, s_sx)

            def rh(th, kc):
                n = th * KC + kc
                buf = n % 2
                v.wait_ge([sem_h0, sem_h1][buf], 16 * (n // 2 + 1))
                if th == 1 and kc == 0:
                    v.wait_ge(sem_t, t_sx + 8)
                hin = hbb[:, buf * TH:(buf + 1) * TH]
                v.tensor_scalar(tmpf[:, buf * TH:(buf + 1) * TH], hin,
                                sxc[:, 0:1], MAGIC, ALU.mult, ALU.add)
                v.tensor_scalar(rhf[:, kc * TH:(kc + 1) * TH],
                                tmpf[:, buf * TH:(buf + 1) * TH],
                                -MAGIC, None, ALU.add)
                V.mark(v)
                v_rhm[n] = V.n

            def out(i):
                par = (i % 2) * D_MODEL
                par3 = (i % 3) * D_MODEL
                pw = psw[i % 3]
                v.wait_ge(sem_t, t_sx + i + 1)
                v.wait_ge(sem_x[i % 3], 16 * (i // 3 + 1))
                if i >= 2:
                    v.wait_ge(sem_so[(i - 2) % 2], 16 * ((i - 2) // 2 + 1))
                v.scalar_tensor_tensor(ot[:, par:par + 512],
                                       pw[:, 0:512], sxc[:, 1:2],
                                       xt[:, par3:par3 + 512],
                                       ALU.mult, ALU.add)
                v.scalar_tensor_tensor(ot[:, par + 512:par + D_MODEL],
                                       pw[:, 512:1024], sxc[:, 1:2],
                                       xt[:, par3 + 512:par3 + D_MODEL],
                                       ALU.mult, ALU.add)
                V.mark(v)
                v_out[i] = V.n

            for kc in range(KC):
                rh(0, kc)
            for i in range(5):
                out(i)
            for kc in range(KC):
                rh(1, kc)
            for i in range(5, 16):
                out(i)
        v_fin = V.n

        @blk.tensor
        def _(t):
            for i in range(16):
                th = i // 8
                pw = psw[i % 3]
                t.wait_ge(sem_v, use5(("rhm", (th + 1) * KC - 1),
                                      v_rhm[(th + 1) * KC - 1]))
                if i >= 3:
                    t.wait_ge(sem_v, use5(("out", i - 3), v_out[i - 3]))
                for nt in range(2):
                    for kc in range(KC):
                        t.matmul(pw[:, nt * 512:nt * 512 + 512],
                                 rhf[:, kc * TH + (i % 8) * 128:
                                     kc * TH + (i % 8) * 128 + 128],
                                 wqb[:, kc * D_MODEL + nt * 512:
                                     kc * D_MODEL + nt * 512 + 512],
                                 start=(kc == 0), stop=False)
                    t.matmul(pw[:, nt * 512:nt * 512 + 512],
                             onesr[0:1, 0:128],
                             brow[0:1, nt * 512:nt * 512 + 512],
                             start=False, stop=True)
                t.drain().then_inc(sem_t, 1)
                TT.n += 1
                t_tile[i] = TT.n
        t_mm = TT.n

        @blk.sync
        def _(sync):
            for th in range(2):
                for kc in range(KC):
                    n = th * KC + kc
                    if n >= 2:
                        sync.wait_ge(sem_v, use5(("rhm", n - 2),
                                                 v_rhm[n - 2]))
                    sync.dma_start(
                        hbb[:, (n % 2) * TH:(n % 2 + 1) * TH],
                        hid[kc * 128:(kc + 1) * 128, th * TH:(th + 1) * TH]
                    ).then_inc([sem_h0, sem_h1][n % 2], 16)

        @blk.gpsimd
        def _(g):
            for i in range(16):
                if i >= 3:
                    g.wait_ge(sem_v, use5(("out", i - 3), v_out[i - 3]))
                g.dma_start(
                    xt[:, (i % 3) * D_MODEL:(i % 3 + 1) * D_MODEL],
                    x2[i * 128:(i + 1) * 128, :]).then_inc(sem_x[i % 3], 16)
                if i >= 2:
                    j = i - 2
                    g.wait_ge(sem_v, use5(("out", j), v_out[j]))
                    g.dma_start(
                        o_out[j * 128:(j + 1) * 128, :],
                        ot[:, (j % 2) * D_MODEL:(j % 2 + 1) * D_MODEL]
                    ).then_inc(sem_so[j % 2], 16)
            for j in (14, 15):
                g.wait_ge(sem_v, use5(("out", j), v_out[j]))
                g.dma_start(
                    o_out[j * 128:(j + 1) * 128, :],
                    ot[:, (j % 2) * D_MODEL:(j % 2 + 1) * D_MODEL]
                ).then_inc(sem_so[j % 2], 16)
            g.wait_ge(sem_so[0], 16 * 8)
            g.wait_ge(sem_so[1], 16 * 8)
        bstk.close()
    return nc


def kernel(**inputs):
    x = np.ascontiguousarray(inputs["L0_attn_x"]).astype(np.float32, copy=False)
    wconv = np.ascontiguousarray(
        inputs["L0_attn_conv_w"].transpose(2, 1, 0).reshape(3 * D_MODEL, D_INNER)
    ).astype(np.float32, copy=False)
    wz = np.ascontiguousarray(inputs["L0_attn_in_proj_w"][:, D_INNER:]).astype(
        np.float32, copy=False)
    wg = np.ascontiguousarray(inputs["L0_attn_gate_w"]).astype(np.float32, copy=False)
    wxp = np.ascontiguousarray(inputs["L0_attn_x_proj_w"]).astype(np.float32, copy=False)
    wo = np.ascontiguousarray(inputs["L0_attn_o_proj_w"]).astype(np.float32, copy=False)
    wo_t = np.ascontiguousarray(
        wo.reshape(16, 128, 1024).transpose(1, 0, 2).reshape(128, 16384))
    bo_r = np.ascontiguousarray(inputs["L0_attn_o_proj_b"][None, :]).astype(np.float32, copy=False)
    w_inT = np.ascontiguousarray(inputs["L0_ffn_W_in"].T).astype(np.float32, copy=False)
    w_out = np.ascontiguousarray(inputs["L0_ffn_W_out"]).astype(np.float32, copy=False)

    biases1 = np.zeros((128, 49), np.float32)
    biases1[:, 0:16] = inputs["L0_attn_conv_b"].reshape(16, 128).T
    biases1[:, 16:32] = inputs["L0_attn_gate_b"].reshape(16, 128).T
    biases1[:, 32:48] = inputs["L0_attn_in_proj_b"][D_INNER:].reshape(16, 128).T
    biases1[0:32, 48] = inputs["L0_attn_x_proj_b"]
    bo_c = np.zeros((128, 8), np.float32)
    bo_c[:, :] = inputs["L0_attn_o_proj_b"].reshape(8, 128).T
    bin_c = np.zeros((128, 32), np.float32)
    bin_c[:, :] = inputs["L0_ffn_b_in"].reshape(32, 128).T
    bout_c = np.zeros((128, 8), np.float32)
    bout_c[:, :] = inputs["L0_ffn_b_out"].reshape(8, 128).T

    identw = np.eye(128, dtype=np.float32)
    identwb = np.eye(128, dtype=BFNP)
    tri = np.zeros((128, 128), np.float32)
    for k in range(127):
        tri[k, k:] = 1.0
    tri[127, :] = 1.0
    tri[:, 127] = 1.0
    tri = tri.astype(BFNP)
    zrowd = np.zeros((1, 16 * 1024), BFNP)
    onesd = np.ones((1, 2048), BFNP)

    nc1 = build_stage1()
    maps1 = []
    for c in range(8):
        b, h = c // 2, c % 2
        t0 = h * T
        if t0 == 0:
            xh = np.concatenate([np.zeros((2, D_MODEL), np.float32), x[b, 0:T]], 0)
        else:
            xh = np.ascontiguousarray(x[b, t0 - 2:t0 + T])
        maps1.append({"xh": xh, "wconv": wconv, "wz": wz, "wg": wg, "wxp": wxp,
                      "biases": biases1, "identw": identw})
    r1 = run_bass_kernel_spmd(nc1, maps1, core_ids=list(range(8))).results

    nc2 = build_stage2()
    maps2 = []
    A_log = np.ascontiguousarray(inputs["L0_attn_A_log"]).astype(np.float32, copy=False)
    Dv = np.ascontiguousarray(inputs["L0_attn_D"]).astype(np.float32, copy=False)
    for c in range(8):
        b, j = c // 2, c % 2
        upad = np.zeros((1024, S + 128), BFNP)
        upad[:, 0:T] = r1[2 * b]["o_u"][j * 1024:(j + 1) * 1024, :]
        upad[:, T:S] = r1[2 * b + 1]["o_u"][j * 1024:(j + 1) * 1024, :]
        bcdt = np.concatenate([r1[2 * b]["o_bcdt"], r1[2 * b + 1]["o_bcdt"]], 1)
        maps2.append({"u_d": upad,
                      "bcdt": np.ascontiguousarray(bcdt),
                      "a_half": np.ascontiguousarray(A_log[j * 1024:(j + 1) * 1024]),
                      "d_half": np.ascontiguousarray(Dv[None, j * 1024:(j + 1) * 1024]),
                      "ltri_c": tri, "zrowd": zrowd, "onesd": onesd,
                      "identw": identw})
    r2 = run_bass_kernel_spmd(nc2, maps2, core_ids=list(range(8))).results

    nc3 = build_stage3()
    maps3 = []
    for c in range(8):
        b, h = c // 2, c % 2
        t0 = h * T
        y_own = np.concatenate([r2[2 * b]["o_y"][t0:t0 + T, :],
                                r2[2 * b + 1]["o_y"][t0:t0 + T, :]], 1)
        maps3.append({"y_own": np.ascontiguousarray(y_own),
                      "sigz": r1[c]["o_sigz"],
                      "x_own": np.ascontiguousarray(x[b, t0:t0 + T]),
                      "wo_t": wo_t, "bo_r": bo_r, "identw": identw})
    r3 = run_bass_kernel_spmd(nc3, maps3, core_ids=list(range(8))).results

    nc4 = build_stage4()
    amax8 = np.concatenate([r3[c]["o_amax"] for c in range(8)], 1)
    maps4 = [{"xn2": r3[c]["o_xn2"], "amax8": amax8, "w_inT": w_inT,
              "bin_c": bin_c, "identw": identw, "identwb": identwb}
             for c in range(8)]
    r4 = run_bass_kernel_spmd(nc4, maps4, core_ids=list(range(8))).results

    nc5 = build_stage5()
    amaxh8 = np.concatenate([r4[c]["o_amax"] for c in range(8)], 1)
    bout_r = np.ascontiguousarray(inputs["L0_ffn_b_out"][None, :]).astype(np.float32, copy=False)
    maps5 = [{"hid": r4[c]["o_hid"], "amax8": amaxh8, "w_out": w_out,
              "bo_row": bout_r, "x2": r3[c]["o_x2"], "identw": identw}
             for c in range(8)]
    r5 = run_bass_kernel_spmd(nc5, maps5, core_ids=list(range(8))).results

    out = np.zeros((B, S, D_MODEL), np.float32)
    for c in range(8):
        b, h = c // 2, c % 2
        out[b, h * T:(h + 1) * T] = r5[c]["o_out"]
    return out

